# version 17
# baseline (speedup 1.0000x reference)
import os
import numpy as np
import ml_dtypes
import concourse.bass as bass
import concourse.mybir as mybir
from concourse.bass_utils import run_bass_kernel_spmd
from contextlib import ExitStack
from collections import defaultdict

F32 = mybir.dt.float32
BF16 = mybir.dt.bfloat16
AF = mybir.ActivationFunctionType
ALU = mybir.AluOpType
AX = mybir.AxisListType

DBG = os.environ.get("KDBG", "0") == "1"
EPS = 1e-5
D = 1024
H = 4096
NT = 16
NP = 16


class Buf:
    __slots__ = ("w", "r")
    def __init__(self):
        self.w = None
        self.r = []


class Em:
    ENGS = ("T", "S", "V", "SY", "GP")
    SEM = {"T": "sT", "S": "sS", "V": "sV", "SY": "sDX", "GP": "sDW"}

    def __init__(self):
        self.q = {e: [] for e in self.ENGS}
        self.cnt = {"sT": 0, "sS": 0, "sV": 0, "sDX": 0, "sDW": 0, "sO": 0, "sGP": 0}
        self.water = {e: {} for e in self.ENGS}
        self.bufs = defaultdict(Buf)

    def op(self, eng, builder, reads=(), writes=(), sem=None):
        own = sem or self.SEM[eng]
        step = 16 if own in ("sDX", "sDW", "sO") else 1
        deps = []
        for n in reads:
            b = self.bufs[n]
            if b.w is not None:
                deps.append(b.w)
        for n in writes:
            b = self.bufs[n]
            deps.extend(b.r)
            if b.w is not None:
                deps.append(b.w)
        waits = {}
        self_edge = eng in ("V", "S", "GP")
        for (s, v) in deps:
            if s == own and not self_edge:
                continue
            if self.water[eng].get(s, 0) >= v:
                continue
            waits[s] = max(waits.get(s, 0), v)
        for s, v in waits.items():
            self.water[eng][s] = v
        self.cnt[own] += step
        tok = (own, self.cnt[own])
        self.q[eng].append((waits, builder, own, step))
        for n in reads:
            self.bufs[n].r.append(tok)
        for n in writes:
            b = self.bufs[n]
            b.w = tok
            b.r = []
        return tok

    def replay(self, eng_name, eng, sems):
        for waits, builder, own, step in self.q[eng_name]:
            for s, v in waits.items():
                eng.wait_ge(sems[s], v)
            builder(eng).then_inc(sems[own], step)


def build_program():
    nc = bass.Bass("TRN2", target_bir_lowering=False, debug=False)

    dt = lambda n, s, ty, k: nc.dram_tensor(n, s, ty, kind=k).ap()
    d_xm = dt("xm", [2048, D], F32, "ExternalInput")
    d_xp = dt("xp", [2048, D], F32, "ExternalInput")
    d_counts = dt("counts", [128, NT], F32, "ExternalInput")
    d_aseed = dt("aseed", [128, NT], F32, "ExternalInput")
    d_w1t = dt("w1t", [D, H], F32, "ExternalInput")
    d_w2t = dt("w2t", [H, D], F32, "ExternalInput")
    d_g1c = dt("g1c", [128, 8], F32, "ExternalInput")
    d_b1vc = dt("b1vc", [128, 8], F32, "ExternalInput")
    d_g2c = dt("g2c", [128, 32], F32, "ExternalInput")
    d_b2vc = dt("b2vc", [128, 32], F32, "ExternalInput")
    d_b1r = dt("b1r", [1, H], F32, "ExternalInput")
    d_b2r = dt("b2r", [1, D], F32, "ExternalInput")
    d_triu = dt("triu", [128, 128], BF16, "ExternalInput")
    d_stril = dt("stril", [128, 128], BF16, "ExternalInput")
    d_ones = dt("ones128", [128, 128], BF16, "ExternalInput")
    d_id = dt("ident", [128, 128], BF16, "ExternalInput")
    d_orow = dt("onesrow", [1, 128], BF16, "ExternalInput")
    d_orowf = dt("onesrowf", [1, 128], F32, "ExternalInput")
    d_idf = dt("identf", [128, 128], F32, "ExternalInput")
    d_ocol = dt("onescol", [128, 1], BF16, "ExternalInput")
    d_out = dt("out", [2048, D], F32, "ExternalOutput")
    if DBG:
        d_dbg_u = dt("dbg_u", [128, D], BF16, "ExternalOutput")
        d_dbg_pa = dt("dbg_pa", [128, D], F32, "ExternalOutput")
        d_dbg_sc = dt("dbg_sc", [128, 16], F32, "ExternalOutput")
        d_dbg_t1 = dt("dbg_t1", [128, 1024], BF16, "ExternalOutput")
        d_dbg_x1 = dt("dbg_x1", [128, D], F32, "ExternalOutput")
        d_dbg_g = dt("dbg_g", [128, H], BF16, "ExternalOutput")
        d_dbg_bt = dt("dbg_bt", [1, H], BF16, "ExternalOutput")
        d_dbg_b2 = dt("dbg_b2", [1, D], BF16, "ExternalOutput")
        d_dbg_t2 = dt("dbg_t2", [128, 1024], BF16, "ExternalOutput")
        d_dbg_th = dt("dbg_th", [1, 8], F32, "ExternalOutput")
        d_dbg_aux = dt("dbg_aux", [2, 128], BF16, "ExternalOutput")
        d_dbg_m1 = dt("dbg_m1", [2, H], BF16, "ExternalOutput")

    em = Em()

    with ExitStack() as stack:
        def sb(name, shape, dtype=F32):
            return stack.enter_context(nc.sbuf_tensor(name, shape, dtype))

        T1 = sb("T1", [128, 8 * H], BF16)
        T2 = sb("T2", [128, 32 * D], BF16)
        scr = sb("scr", [128, 2048], F32)
        stg = [scr[:, 0:1024], scr[:, 1024:2048]]
        x1cT = scr[:, 0:512].bitcast(BF16)

        x0b = [sb(f"x0b{i}", [128, D], F32) for i in range(2)]
        x1b = [sb(f"x1b{i}", [128, D], F32) for i in range(2)]
        ub = [sb(f"ub{i}", [128, D], BF16) for i in range(3)]
        x1c = [sb(f"x1c{i}", [128, D], BF16) for i in range(2)]
        sqjunk = scr[:, :].bitcast(BF16)
        alph = [sb(f"alph{i}", [128, 1], F32) for i in range(2)]
        auxc = [sb(f"auxc{i}", [128, 2], F32) for i in range(2)]
        auxc2 = sb("auxc2", [128, 2], F32)
        auxL1 = [sb(f"auxL1{i}", [2, 128], BF16) for i in range(2)]
        auxL2 = sb("auxL2", [2, 128], BF16)
        m1mov = sb("m1mov", [2, H], BF16)
        m2mov = sb("m2mov", [2, D], BF16)
        gb = sb("gb", [128, H], BF16)
        bfs = sb("bfs", [128, 2048], BF16)
        absb = bfs[:, 0:1024]
        acopy = bfs[:, 0:1024]
        absb2 = bfs[:, 1024:1536]
        fcopy = bfs[:, 1024:2048]
        zT = sb("zT", [128, H], BF16)
        t3 = sb("t3", [128, D], F32)
        t3f = sb("t3f", [128, D], F32)
        c_triu = sb("c_triu", [128, 128], BF16)
        c_stril = sb("c_stril", [128, 128], BF16)
        c_ones = sb("c_ones", [128, 128], BF16)
        c_id = sb("c_id", [128, 128], BF16)
        c_orow = sb("c_orow", [1, 128], BF16)
        c_orowf = sb("c_orowf", [1, 128], F32)
        c_idf = sb("c_idf", [128, 128], F32)
        s_lv = [sb(f"s_lv{i}", [128, 32], F32) for i in range(2)]
        s_lvr = sb("s_lvr", [128, 32], F32)
        s_acc = sb("s_acc", [128, 1], F32)
        s_red = sb("s_red", [1, 128], F32)
        s_Lt = sb("s_Lt", [1, 1], F32)
        s_Lpp = sb("s_Lpp", [128, 1], F32)
        s_R = sb("s_R", [1, 1], F32)
        c_ocol = sb("c_ocol", [128, 1], BF16)
        s_counts = sb("s_counts", [128, NT], F32)
        s_aseed = sb("s_aseed", [128, NT], F32)
        s_rs = sb("s_rs", [128, NT], F32)
        s_rs2 = sb("s_rs2", [128, NT], F32)
        s_g1c = sb("s_g1c", [128, 8], F32)
        s_b1vc = sb("s_b1vc", [128, 8], F32)
        s_b1vcb = sb("s_b1vcb", [128, 8], BF16)
        s_g2c = sb("s_g2c", [128, 32], F32)
        s_b2vc = sb("s_b2vc", [128, 32], F32)
        s_b2vcb = sb("s_b2vcb", [128, 32], BF16)
        s_brow = scr[0:1, 0:1024]
        s_g1q = sb("s_g1q", [128, 8], F32)
        s_g2q = sb("s_g2q", [128, 32], F32)
        s_bn = sb("s_bn", [128, 12], F32)
        s_bnA = sb("s_bnA", [128, 12], F32)
        gsc = {k: sb(f"gsc_{k}", [128, 1], F32) for k in
               ("ms", "M2s", "msqs", "va", "x", "t1", "t2",
                "mu", "mua", "inv0", "nb0", "inva", "sA", "bA")}
        s_gmsq = sb("s_gmsq", [128, 4], F32)
        gaccq = [sb(f"gaccq{i}", [128, 1], F32) for i in range(8)]
        s_gsq = sb("s_gsq", [128, 1], F32)
        sc = {k: sb(f"sc_{k}", [128, 1], F32) for k in
              ("ms", "mu", "M2s", "msqs", "va", "vb", "x", "t1", "t2", "t3",
               "t4", "inv0", "nb0", "inva", "sA", "bA", "inv1",
               "inv2", "invf", "bF", "mua", "mu2", "muf", "mu1")}
        s_msq = sb("s_msq", [128, 4], F32)
        bab = {k: sb(f"bab_{k}", [128, 1], F32) for k in
               ("pre", "x0", "a", "x1", "g", "f")}
        s_th1 = sb("s_th1", [1, 1], F32)
        s_q1 = sb("s_q1", [1, 1], F32)
        s_th2 = sb("s_th2", [1, 1], F32)
        s_q2 = sb("s_q2", [1, 1], F32)
        s_m1 = sb("s_m1", [1, 1], F32)
        s_m2 = sb("s_m2", [1, 1], F32)
        s_thpp = sb("s_thpp", [128, 1], F32)
        s_nthpp = sb("s_nthpp", [128, 1], F32)
        s_thpp2 = sb("s_thpp2", [128, 1], F32)
        s_nthpp2 = sb("s_nthpp2", [128, 1], F32)
        s_q1pp = sb("s_q1pp", [128, 1], F32)
        s_q2pp = sb("s_q2pp", [128, 1], F32)
        if DBG:
            sdbg_sc = sb("sdbg_sc", [128, 16], F32)
            sdbg_th = sb("sdbg_th", [1, 8], F32)

        pA = stack.enter_context(nc.psum_tensor("pA", [128, 1024], F32))
        ptr = stack.enter_context(nc.psum_tensor("ptr", [128, 1024], BF16))
        pv = [stack.enter_context(nc.psum_tensor(f"pv{i}", [128, 512], F32))
              for i in range(2)]
        pF = stack.enter_context(nc.psum_tensor("pF", [128, 1024], F32))
        pX = stack.enter_context(nc.psum_tensor("pX", [128, 512], F32))

        sems_names = ("sT", "sS", "sV", "sDX", "sDW", "sO", "sGP")

        V = lambda b, r, w: em.op("V", b, r, w)
        S = lambda b, r, w: em.op("S", b, r, w)
        T = lambda b, r, w: em.op("T", b, r, w)
        SY = lambda b, r, w, sem="sDX": em.op("SY", b, r, w, sem=sem)
        GP = lambda b, r, w: em.op("GP", b, r, w)
        GC = lambda b, r, w: em.op("GP", b, r, w, sem="sGP")

        def dma(dstap, srcap):
            return lambda e: e.dma_start(out=dstap, in_=srcap)

        for dst, src in ((c_triu[:], d_triu), (c_stril[:], d_stril),
                         (c_ones[:], d_ones), (c_id[:], d_id),
                         (c_orow[:], d_orow), (c_orowf[:], d_orowf),
                         (c_idf[:], d_idf),
                         (c_ocol[:], d_ocol), (s_counts[:], d_counts),
                         (s_aseed[:], d_aseed),
                         (s_g1c[:], d_g1c), (s_b1vc[:], d_b1vc),
                         (s_g2c[:], d_g2c), (s_b2vc[:], d_b2vc)):
            SY(dma(dst, src), [], ["consts"])

        V(lambda e: e.reciprocal(s_rs[:], s_counts[:]), ["consts"], ["rs"])
        V(lambda e: e.tensor_mul(s_rs2[:], s_rs[:], s_rs[:]), ["rs"], ["rs2"])
        V(lambda e: e.tensor_copy(s_b1vcb[:], s_b1vc[:]), ["consts"], ["b1vcb"])
        V(lambda e: e.tensor_copy(s_b2vcb[:], s_b2vc[:]), ["consts"], ["b2vcb"])
        V(lambda e: [e.memset(bab[kk][:], 0.7 if kk == "x1" else 1.0)
                     for kk in bab][-1], [], [f"bab_{kk2}" for kk2 in
                     ("pre", "x0", "a", "x1", "g", "f")])

        def p1_w1(j):
            dj, qq = divmod(j, 4)
            srcap = d_w1t[128 * dj:128 * (dj + 1), 1024 * qq:1024 * (qq + 1)]
            GP(dma(stg[j % 2], srcap), [], [f"scr{j % 2}"])
            S((lambda s=stg[j % 2], j=j: (lambda e: e.activation(
                absb[:], s[:], AF.Abs, accum_out=s_lv[0][:, j:j + 1])))(),
              [f"scr{j % 2}"], ["absb", "lv0"])

        def p1_w2(j):
            srcap = d_w2t[128 * j:128 * (j + 1), :]
            GP(dma(stg[j % 2], srcap), [], [f"scr{j % 2}"])
            S((lambda s=stg[j % 2], j=j: (lambda e: e.activation(
                absb[:], s[:], AF.Abs, accum_out=s_lv[1][:, j:j + 1])))(),
              [f"scr{j % 2}"], ["absb", "lv1"])

        def p2_w1(j):
            dj, qq = divmod(j, 4)
            srcap = d_w1t[128 * dj:128 * (dj + 1), 1024 * qq:1024 * (qq + 1)]
            dstsl = T1[:, dj * H + 1024 * qq: dj * H + 1024 * (qq + 1)]
            GP(dma(stg[j % 2], srcap), [], [f"scr{j % 2}"])
            V((lambda s=stg[j % 2]: (lambda e: e.tensor_scalar(
                absb[:], s[:], s_thpp[:], 1.0, op0=ALU.is_ge, op1=ALU.subtract)))(),
              [f"scr{j % 2}", "thpp"], ["absb"])
            V((lambda s=stg[j % 2], dst2=dstsl: (lambda e: e.scalar_tensor_tensor(
                dst2, s[:], s_nthpp[:], absb[:], op0=ALU.is_ge, op1=ALU.add)))(),
              [f"scr{j % 2}", "nthpp", "absb"], ["T1"])

        def p2_w2(j):
            srcap = d_w2t[128 * j:128 * (j + 1), :]
            dstsl = T2[:, j * D:(j + 1) * D]
            GP(dma(stg[j % 2], srcap), [], [f"scr{j % 2}"])
            V((lambda s=stg[j % 2]: (lambda e: e.tensor_scalar(
                absb[:], s[:], s_thpp2[:], 1.0, op0=ALU.is_ge, op1=ALU.subtract)))(),
              [f"scr{j % 2}", "thpp2"], ["absb"])
            V((lambda s=stg[j % 2], dst2=dstsl: (lambda e: e.scalar_tensor_tensor(
                dst2, s[:], s_nthpp2[:], absb[:], op0=ALU.is_ge, op1=ALU.add)))(),
              [f"scr{j % 2}", "nthpp2", "absb"], ["T2"])

        NW = float(H * D)
        def hi_mean(i, th_t, q_t, thk, qk, thpp_t, nthpp_t, qpp_t, gq_t, gc_t,
                    thppk, nthppk, qppk, gqk):
            lvk = f"lv{i}"
            V((lambda i=i: (lambda e: e.tensor_reduce(s_acc[:], s_lv[i][:], AX.X, ALU.add)))(),
              [lvk, "absb"], ["hacc"])
            T(lambda e: e.matmul(pF[0:1, 0:128], s_acc[:], c_idf[:], start=True, stop=True),
              ["hacc", "consts"], ["pF"])
            V(lambda e: e.tensor_copy(s_red[:], pF[0:1, 0:128]), ["pF"], ["red"])
            for wl in (64, 32, 16, 8, 4, 2, 1):
                V((lambda wl=wl: (lambda e: e.tensor_add(
                    s_red[0:1, 0:wl], s_red[0:1, 0:wl], s_red[0:1, wl:2 * wl])))(),
                  ["red"], ["red"])
            V(lambda e: e.tensor_scalar_mul(s_Lt[:], s_red[0:1, 0:1], 1.0 / 4096.0), ["red"], ["Lt"])
            T(lambda e: e.matmul(pX[:, 0:1], c_orowf[:], s_Lt[:], start=True, stop=True),
              ["Lt", "consts"], ["pX"])
            V(lambda e: e.tensor_copy(s_Lpp[:], pX[:, 0:1]), ["pX"], ["Lpp"])
            V((lambda i=i: (lambda e: e.tensor_scalar(
                s_lvr[:], s_lv[i][:], s_Lpp[:], None, op0=ALU.subtract)))(),
              [lvk, "Lpp"], ["lvr"])
            V(lambda e: e.tensor_reduce(s_acc[:], s_lvr[:], AX.X, ALU.add), ["lvr"], ["hacc"])
            T(lambda e: e.matmul(pF[0:1, 0:128], s_acc[:], c_idf[:], start=True, stop=True),
              ["hacc", "consts"], ["pF"])
            V(lambda e: e.tensor_copy(s_red[:], pF[0:1, 0:128]), ["pF"], ["red"])
            for wl in (64, 32, 16, 8, 4, 2, 1):
                V((lambda wl=wl: (lambda e: e.tensor_add(
                    s_red[0:1, 0:wl], s_red[0:1, 0:wl], s_red[0:1, wl:2 * wl])))(),
                  ["red"], ["red"])
            V(lambda e: e.tensor_scalar_mul(s_m1[:], s_Lt[:], 2.0 ** -10), ["Lt"], ["m1"])
            V(lambda e: e.tensor_scalar_mul(s_m2[:], s_red[0:1, 0:1], 2.0 ** -22), ["red"], ["m2"])
            V(lambda e: e.tensor_add(s_m1[:], s_m1[:], s_m2[:]), ["m1", "m2"], ["m1"])
            V(lambda e: e.tensor_scalar_max(s_m2[:], s_m1[:], 1e-5), ["m1"], ["m2"])
            V((lambda t=th_t: (lambda e: e.reciprocal(t[:], s_m2[:])))(), ["m2"], [thk])
            V((lambda t=th_t, u=q_t: (lambda e: e.reciprocal(u[:], t[:])))(), [thk], [qk])
            V((lambda t=th_t, u=q_t: (lambda e: e.tensor_scalar_mul(t[:], u[:], 0.5)))(), [qk], [thk])
            T((lambda t=th_t: (lambda e: e.matmul(pX[:, 1:2], c_orowf[:], t[:], start=True, stop=True)))(),
              [thk, "consts"], ["pX"])
            V((lambda t=thpp_t: (lambda e: e.tensor_copy(t[:], pX[:, 1:2])))(), ["pX"], [thppk])
            V((lambda t=nthpp_t: (lambda e: e.tensor_scalar_mul(t[:], pX[:, 1:2], -1.0)))(), ["pX"], [nthppk])
            V((lambda t=qpp_t, u=thpp_t: (lambda e: e.tensor_scalar_mul(t[:], u[:], 2.0)))(), [thppk], [qppk])
            V((lambda t=gq_t, u=gc_t, v2=qpp_t: (lambda e: e.tensor_scalar_mul(t[:], u[:], v2[:])))(),
              ["consts", qppk], [gqk])

        def combine1024(mu_key, rs_slice=None, rs2_slice=None, E=None, scd=None, P="", msqt=None, bnk="bn", bnt=None):
            useV = E is None or E is V
            if E is None:
                E = V
            if scd is None:
                scd = sc
            if msqt is None:
                msqt = s_msq
            if bnt is None:
                bnt = s_bn
            mu = scd[mu_key]
            if useV:
                E(lambda e: e.tensor_reduce(scd["ms"][:], bnt[:, 1::3], AX.X, ALU.add), [bnk], [P + "ms"])
                E(lambda e: e.tensor_mul(msqt[:], bnt[:, 1::3], bnt[:, 1::3]), [bnk], [P + "msq"])
                E(lambda e: e.tensor_reduce(scd["msqs"][:], msqt[:], AX.X, ALU.add), [P + "msq"], [P + "msqs"])
                E(lambda e: e.tensor_reduce(scd["M2s"][:], bnt[:, 2::3], AX.X, ALU.add), [bnk], [P + "M2s"])
            else:
                E(lambda e: e.tensor_add(msqt[:, 0:1], bnt[:, 1:2], bnt[:, 4:5]), [bnk], [P + "msq"])
                E(lambda e: e.tensor_add(msqt[:, 1:2], bnt[:, 7:8], bnt[:, 10:11]), [bnk], [P + "msq"])
                E(lambda e: e.tensor_add(scd["ms"][:], msqt[:, 0:1], msqt[:, 1:2]), [P + "msq"], [P + "ms"])
                E(lambda e: e.tensor_mul(msqt[:, 0:1], bnt[:, 1:2], bnt[:, 1:2]), [bnk], [P + "msq"])
                E(lambda e: e.tensor_mul(msqt[:, 1:2], bnt[:, 4:5], bnt[:, 4:5]), [bnk], [P + "msq"])
                E(lambda e: e.tensor_mul(msqt[:, 2:3], bnt[:, 7:8], bnt[:, 7:8]), [bnk], [P + "msq"])
                E(lambda e: e.tensor_mul(msqt[:, 3:4], bnt[:, 10:11], bnt[:, 10:11]), [bnk], [P + "msq"])
                E(lambda e: e.tensor_add(msqt[:, 0:1], msqt[:, 0:1], msqt[:, 1:2]), [P + "msq"], [P + "msq"])
                E(lambda e: e.tensor_add(msqt[:, 2:3], msqt[:, 2:3], msqt[:, 3:4]), [P + "msq"], [P + "msq"])
                E(lambda e: e.tensor_add(scd["msqs"][:], msqt[:, 0:1], msqt[:, 2:3]), [P + "msq"], [P + "msqs"])
                E(lambda e: e.tensor_add(msqt[:, 0:1], bnt[:, 2:3], bnt[:, 5:6]), [bnk], [P + "msq"])
                E(lambda e: e.tensor_add(msqt[:, 1:2], bnt[:, 8:9], bnt[:, 11:12]), [bnk], [P + "msq"])
                E(lambda e: e.tensor_add(scd["M2s"][:], msqt[:, 0:1], msqt[:, 1:2]), [P + "msq"], [P + "M2s"])
            E(lambda e: e.tensor_scalar_mul(mu[:], scd["ms"][:], 0.25), [P + "ms"], [P + mu_key])
            E(lambda e: e.tensor_mul(scd["t1"][:], mu[:], mu[:]), [P + mu_key], [P + "t1"])
            E(lambda e: e.tensor_scalar_mul(scd["va"][:], scd["M2s"][:], 1.0 / 1024.0), [P + "M2s"], [P + "va"])
            if useV:
                E(lambda e: e.scalar_tensor_tensor(scd["va"][:], scd["msqs"][:], 0.25, scd["va"][:],
                                                   op0=ALU.mult, op1=ALU.add), [P + "msqs", P + "va"], [P + "va"])
            else:
                E(lambda e: e.tensor_scalar_mul(scd["t2"][:], scd["msqs"][:], 0.25), [P + "msqs"], [P + "t2"])
                E(lambda e: e.tensor_add(scd["va"][:], scd["va"][:], scd["t2"][:]), [P + "va", P + "t2"], [P + "va"])
            if rs_slice is not None:
                E((lambda r2=rs_slice: (lambda e: e.tensor_mul(mu[:], mu[:], r2)))(),
                  [P + mu_key, "rs"], [P + mu_key])
                E(lambda e: e.tensor_sub(scd["va"][:], scd["va"][:], scd["t1"][:]), [P + "va", P + "t1"], [P + "va"])
                E((lambda r2=rs2_slice: (lambda e: e.tensor_mul(scd["va"][:], scd["va"][:], r2)))(),
                  [P + "va", "rs2"], [P + "va"])
                E(lambda e: e.tensor_scalar(scd["x"][:], scd["va"][:], 0.0, EPS, op0=ALU.max, op1=ALU.add), [P + "va"], [P + "x"])
            else:
                E(lambda e: e.tensor_sub(scd["va"][:], scd["va"][:], scd["t1"][:]), [P + "va", P + "t1"], [P + "va"])
                E(lambda e: e.tensor_scalar(scd["x"][:], scd["va"][:], 0.0, EPS, op0=ALU.max, op1=ALU.add), [P + "va"], [P + "x"])
        def babylon(state_key, iters, inv_key, seed=None, E=None, scd=None, P=""):
            if E is None:
                E = V
            if scd is None:
                scd = sc
            st = bab[state_key]
            bk = "bab_" + state_key
            if seed is not None:
                E((lambda s2=seed: (lambda e: e.tensor_copy(st[:], s2)))(),
                  ["consts"], [bk])
            for _ in range(iters):
                E(lambda e: e.tensor_mul(scd["t2"][:], st[:], st[:]), [bk], [P + "t2"])
                E(lambda e: e.tensor_mul(scd["t2"][:], scd["x"][:], scd["t2"][:]),
                  [P + "x", P + "t2"], [P + "t2"])
                E(lambda e: e.tensor_scalar(scd["t2"][:], scd["t2"][:], -0.5, 1.5,
                                            op0=ALU.mult, op1=ALU.add), [P + "t2"], [P + "t2"])
                E(lambda e: e.tensor_mul(st[:], st[:], scd["t2"][:]), [bk, P + "t2"], [bk])
            E((lambda t=scd[inv_key]: (lambda e: e.tensor_copy(t[:], st[:])))(), [bk], [P + inv_key])
        def babylon_sqrt(state_key, iters, inv_key):
            st = bab[state_key]
            bk = "bab_" + state_key
            for _ in range(iters):
                V(lambda e: e.reciprocal(sc["t2"][:], st[:]), [bk], ["t2"])
                V(lambda e: e.tensor_mul(sc["t2"][:], sc["x"][:], sc["t2"][:]), ["x", "t2"], ["t2"])
                V(lambda e: e.tensor_add(sc["t2"][:], st[:], sc["t2"][:]), [bk, "t2"], ["t2"])
                V(lambda e: e.tensor_scalar_mul(st[:], sc["t2"][:], 0.5), ["t2"], [bk])
            V((lambda t=sc[inv_key]: (lambda e: e.reciprocal(t[:], st[:])))(), [bk], [inv_key])

        def pa_stats(k):
            main = k >= NP
            kk = k - NP
            par = k % 2
            xsrc = d_xm[128 * kk:128 * (kk + 1), :] if main else d_xp[128 * k:128 * (k + 1), :]
            xk = f"x0b{par}"
            SY((lambda dst2=x0b[par][:], s2=xsrc: (lambda e: e.dma_start(out=dst2, in_=s2)))(),
               [], [xk])
            V((lambda par=par: (lambda e: e.bn_stats(s_bnA[:, 0:6], x0b[par][:, 0:512])))(),
              [xk], ["bnA"])
            V((lambda par=par: (lambda e: e.bn_stats(s_bnA[:, 6:12], x0b[par][:, 512:1024])))(),
              [xk], ["bnA"])
            combine1024("mu", E=GC, scd=gsc, P="g", msqt=s_gmsq, bnk="bnA", bnt=s_bnA)
            babylon("pre" if not main else "x0", 6 if k == 0 or kk == 0 else 3, "inv0",
                    E=GC, scd=gsc, P="g")
            GC(lambda e: e.tensor_scalar_mul(gsc["t1"][:], gsc["mu"][:], -1.0), ["gmu"], ["gt1"])
            GC(lambda e: e.tensor_mul(gsc["nb0"][:], gsc["t1"][:], gsc["inv0"][:]), ["gt1", "ginv0"], ["gnb0"])
            u3 = k % 3
            S((lambda par=par, u3=u3: (lambda e: e.activation(
                ub[u3][:], x0b[par][:], AF.Identity, bias=gsc["nb0"][:], scale=gsc["inv0"][:])))(),
              [xk, "gnb0", "ginv0"], [f"ub{u3}"])

        def pa_mms(k):
            main = k >= NP
            kk = k - NP
            u3 = k % 3
            p3 = (k - 1) % 3
            if not main:
                T((lambda u3=u3, first=(k == 0): (lambda e: (
                    e.matmul(pA[:, 0:512], c_ones[:], ub[u3][:, 0:512], start=first, stop=False),
                    e.matmul(pA[:, 512:1024], c_ones[:], ub[u3][:, 512:1024], start=first, stop=False))[-1]))(),
                  [f"ub{u3}", "consts"], ["pA"])
            else:
                T((lambda u3=u3, p3=p3, prom=(kk > 0): (lambda e: (
                    ([e.matmul(pA[:, 0:512], c_stril[:], ub[p3][:, 0:512], start=False, stop=False),
                      e.matmul(pA[:, 512:1024], c_stril[:], ub[p3][:, 512:1024], start=False, stop=False)]
                     if prom else []),
                    e.matmul(pA[:, 0:512], c_triu[:], ub[u3][:, 0:512], start=False, stop=False),
                    e.matmul(pA[:, 512:1024], c_triu[:], ub[u3][:, 512:1024], start=False, stop=False))[-1]))(),
                  [f"ub{u3}", f"ub{p3}", "consts"], ["pA"])

        def pa_post(k):
            kk = k - NP
            par = k % 2
            xk = f"x0b{par}"
            S((lambda kk=kk: (lambda e: e.activation(
                acopy, pA[:], AF.Identity, scale=s_rs[:, kk:kk + 1])))(),
              ["pA", "rs"], ["absb"])
            V(lambda e: e.bn_stats(s_bnA[:, 0:6], acopy[:, 0:512]), ["absb"], ["bnA"])
            V(lambda e: e.bn_stats(s_bnA[:, 6:12], acopy[:, 512:1024]), ["absb"], ["bnA"])
            combine1024("mua", E=GC, scd=gsc, P="g", msqt=s_gmsq, bnk="bnA", bnt=s_bnA)
            babylon("a", 3, "inva", seed=s_aseed[:, kk:kk + 1], E=GC, scd=gsc, P="g")
            GC(lambda e: e.tensor_scalar_mul(gsc["t1"][:], gsc["mua"][:], -1.0), ["gmua"], ["gt1"])
            GC(lambda e: e.tensor_mul(gsc["bA"][:], gsc["t1"][:], gsc["inva"][:]), ["gt1", "ginva"], ["gbA"])
            S(lambda e: e.activation(t3[:], acopy, AF.Identity,
                                     bias=gsc["bA"][:], scale=gsc["inva"][:]),
              ["absb", "gbA", "ginva"], ["t3buf"])
            V((lambda par=par: (lambda e: e.tensor_add(x1b[par][:], x0b[par][:], t3[:])))(),
              [xk, "t3buf"], [f"x1b{par}"])
            if DBG and kk == 0:
                SY((lambda par=par: (lambda e: e.dma_start(out=d_dbg_x1, in_=x1b[par][:])))(),
                   [f"x1b{par}"], ["dbg5"], sem="sO")

        def x1prep(k):
            kk = k - NP
            par = k % 2
            S((lambda par=par: (lambda e: e.activation(x1c[par][:], x1b[par][:], AF.Identity)))(),
              [f"x1b{par}"], [f"x1c{par}"])
            V((lambda par=par: (lambda e: e.bn_stats(s_bn[:, 0:6], x1b[par][:, 0:512])))(),
              [f"x1b{par}"], ["bn"])
            V((lambda par=par: (lambda e: e.bn_stats(s_bn[:, 6:12], x1b[par][:, 512:1024])))(),
              [f"x1b{par}"], ["bn"])
            combine1024("mu1")
            babylon("x1", 6 if kk == 0 else 3, "inv1")
            V(lambda e: e.scalar_tensor_tensor(sc["t1"][:], sc["inv1"][:], 1.0, sc["inv1"][:],
                                               op0=ALU.mult, op1=ALU.mult), ["inv1"], ["t1"])
            V(lambda e: e.tensor_mul(sc["vb"][:], sc["va"][:], sc["t1"][:]), ["va", "t1"], ["vb"])
            V(lambda e: e.tensor_scalar(sc["t4"][:], sc["vb"][:], -0.5, 1.5 - 0.5 * EPS,
                                        op0=ALU.mult, op1=ALU.add), ["vb"], ["t4"])
            V((lambda par=par: (lambda e: e.tensor_mul(alph[par][:], sc["inv1"][:], sc["t4"][:])))(),
              ["inv1", "t4"], [f"al{par}"])
            V((lambda par=par: (lambda e: e.reciprocal(auxc[par][:, 0:1], alph[par][:])))(),
              [f"al{par}"], [f"auxc{par}"])
            V((lambda par=par: (lambda e: e.tensor_scalar_mul(auxc[par][:, 1:2], sc["mu1"][:], -1.0)))(),
              ["mu1"], [f"auxc{par}"])
            T((lambda par=par: (lambda e: e.matmul(pX[0:2, 0:128], auxc[par][:], c_idf[:],
                                                   start=True, stop=True)))(),
              [f"auxc{par}", "consts"], ["pX"])
            V((lambda par=par: (lambda e: e.tensor_copy(auxL1[par][:], pX[0:2, 0:128])))(),
              ["pX"], [f"auxL1{par}"])

        def ffn_heavy(k):
            kk = k - NP
            par = k % 2
            T((lambda par=par: (lambda e: [e.transpose(ptr[:, 128 * jj:128 * (jj + 1)],
                                           x1c[par][:, 128 * jj:128 * (jj + 1)], c_id[:])
                                           for jj in range(8)][-1]))(),
              [f"x1c{par}", "consts"], ["ptr"])
            V(lambda e: e.tensor_copy(x1cT, ptr[:]), ["ptr"], ["scr0"])

            for q in range(8):
                pb = q % 2
                T((lambda q=q, pb=pb, par=par: (lambda e: (
                    [e.matmul(pv[pb][:], x1cT[:, 128 * jj:128 * (jj + 1)],
                              T1[:, jj * H + 512 * q: jj * H + 512 * (q + 1)],
                              start=(jj == 0), stop=False) for jj in range(8)],
                    e.matmul(pv[pb][:], auxL1[par][:],
                             m1mov[0:2, 512 * q:512 * (q + 1)],
                             start=False, stop=True))[-1]))(),
                  ["scr0", "T1", "m1movb", f"auxL1{par}", "consts"], [f"pv{pb}"])
                S((lambda q=q, pb=pb, par=par: (lambda e: e.activation(
                    gb[:, 512 * q:512 * (q + 1)], pv[pb][:], AF.Gelu,
                    scale=alph[par][:], accum_out=gaccq[q][:])))(),
                  [f"pv{pb}", f"al{par}"], ["gbuf", f"gq{q}"])
                if q % 2 == 1:
                    grp = q // 2
                    T((lambda grp=grp: (lambda e: [e.transpose(
                        ptr[:, 128 * jj:128 * (jj + 1)],
                        gb[:, 1024 * grp + 128 * jj: 1024 * grp + 128 * (jj + 1)], c_id[:])
                        for jj in range(8)][-1]))(), ["gbuf", "consts"], ["ptr"])
                    V((lambda grp=grp: (lambda e: e.tensor_copy(
                        zT[:, 1024 * grp:1024 * (grp + 1)], ptr[:])))(), ["ptr"], ["zTb"])

            if DBG and kk == 0:
                SY(lambda e: e.dma_start(out=d_dbg_g, in_=gb[:]), ["gbuf"], ["dbg6"], sem="sO")
                V((lambda par=par: (lambda e: (
                    e.tensor_copy(sdbg_sc[:, 0:1], alph[par][:]),
                    e.tensor_copy(sdbg_sc[:, 1:2], auxc[par][:, 0:1]),
                    e.tensor_copy(sdbg_sc[:, 2:3], auxc[par][:, 1:2]),
                    e.tensor_copy(sdbg_sc[:, 3:4], sc["inv1"][:]),
                    e.tensor_copy(sdbg_sc[:, 4:5], sc["mu1"][:]))[-1]))(),
                  [f"al{par}", f"auxc{par}", "inv1", "mu1"], ["dbgsc"])
                SY(lambda e: e.dma_start(out=d_dbg_sc, in_=sdbg_sc[:]), ["dbgsc"], ["dbg7"], sem="sO")
            S(lambda e: e.activation(sqjunk, gb[:], AF.Square, accum_out=s_gsq[:]),
              ["gbuf"], ["scr0", "scr1", "gsq"])
            for hh in range(2):
                T((lambda hh=hh: (lambda e: [e.matmul(
                    pF[:, 512 * hh:512 * (hh + 1)],
                    zT[:, 128 * cc:128 * (cc + 1)],
                    T2[:, cc * D + 512 * hh: cc * D + 512 * (hh + 1)],
                    start=(cc == 0), stop=False) for cc in range(32)][-1]))(),
                  ["zTb", "T2", "consts"], ["pF"])
            V(lambda e: e.tensor_add(s_msq[:, 0:1], gaccq[0][:], gaccq[1][:]), ["gq0", "gq1", "gsq"], ["msq"])
            V(lambda e: e.tensor_add(s_msq[:, 1:2], gaccq[2][:], gaccq[3][:]), ["gq2", "gq3"], ["msq"])
            V(lambda e: e.tensor_add(s_msq[:, 2:3], gaccq[4][:], gaccq[5][:]), ["gq4", "gq5"], ["msq"])
            V(lambda e: e.tensor_add(s_msq[:, 3:4], gaccq[6][:], gaccq[7][:]), ["gq6", "gq7"], ["msq"])
            V(lambda e: e.tensor_add(s_msq[:, 0:1], s_msq[:, 0:1], s_msq[:, 1:2]), ["msq"], ["msq"])
            V(lambda e: e.tensor_add(s_msq[:, 2:3], s_msq[:, 2:3], s_msq[:, 3:4]), ["msq"], ["msq"])
            V(lambda e: e.tensor_add(sc["ms"][:], s_msq[:, 0:1], s_msq[:, 2:3]), ["msq"], ["ms"])
            V(lambda e: e.tensor_scalar_mul(sc["mu2"][:], sc["ms"][:], 1.0 / 4096.0), ["ms"], ["mu2"])
            V(lambda e: e.tensor_mul(sc["t1"][:], sc["mu2"][:], sc["mu2"][:]), ["mu2"], ["t1"])
            V(lambda e: e.scalar_tensor_tensor(sc["va"][:], s_gsq[:], 1.0 / 4096.0, sc["t1"][:],
                                               op0=ALU.mult, op1=ALU.subtract), ["gsq", "t1"], ["va"])
            V(lambda e: e.tensor_scalar(sc["x"][:], sc["va"][:], 0.0, EPS, op0=ALU.max, op1=ALU.add), ["va"], ["x"])
            babylon_sqrt("g", 6 if kk == 0 else 4, "inv2")
            if DBG and kk == 0:
                V(lambda e: (
                    e.tensor_copy(sdbg_sc[:, 5:6], sc["inv2"][:]),
                    e.tensor_copy(sdbg_sc[:, 6:7], sc["mu2"][:]),
                    e.tensor_copy(sdbg_sc[:, 7:8], bab["g"][:]),
                    e.tensor_copy(sdbg_sc[:, 8:9], sc["x"][:]),
                    e.tensor_copy(sdbg_sc[:, 9:10], sc["va"][:]),
                    e.tensor_copy(sdbg_sc[:, 10:11], s_gsq[:]),
                    e.tensor_copy(sdbg_sc[:, 11:12], gaccq[0][:]))[-1],
                  ["inv2", "mu2", "bab_g", "x", "va", "gsq", "gq0"], ["dbgsc"])
            if DBG and kk == 0:
                SY((lambda par=par: (lambda e: e.dma_start(out=d_dbg_aux, in_=auxL1[par][:])))(),
                   [f"auxL1{par}"], ["dbga1"], sem="sO")
                SY(lambda e: e.dma_start(out=d_dbg_m1, in_=m1mov[:]), ["m1movb"], ["dbga2"], sem="sO")
            V(lambda e: e.tensor_copy(auxc2[:, 0:1], bab["g"][:]), ["bab_g"], ["auxc2"])
            V(lambda e: e.tensor_scalar_mul(auxc2[:, 1:2], sc["mu2"][:], -1.0), ["mu2"], ["auxc2"])
            T(lambda e: e.matmul(pX[0:2, 128:256], auxc2[:], c_idf[:], start=True, stop=True),
              ["auxc2", "consts"], ["pX"])
            V(lambda e: e.tensor_copy(auxL2[:], pX[0:2, 128:256]), ["pX"], ["auxL2b"])
            for hh in range(2):
                T((lambda hh=hh: (lambda e: e.matmul(
                    pF[:, 512 * hh:512 * (hh + 1)], auxL2[:],
                    m2mov[0:2, 512 * hh:512 * (hh + 1)],
                    start=False, stop=True)))(),
                  ["auxL2b", "m2movb", "consts"], ["pF"])

        def ffn_tail(k):
            kk = k - NP
            par = k % 2
            S(lambda e: e.activation(fcopy, pF[:], AF.Identity), ["pF"], ["fcop"])
            V(lambda e: e.bn_stats(s_bn[:, 0:6], fcopy[:, 0:512]), ["fcop"], ["bn"])
            V(lambda e: e.bn_stats(s_bn[:, 6:12], fcopy[:, 512:1024]), ["fcop"], ["bn"])
            combine1024("muf")
            V(lambda e: e.tensor_mul(sc["muf"][:], sc["muf"][:], sc["inv2"][:]), ["muf", "inv2"], ["muf"])
            V(lambda e: e.tensor_mul(sc["t1"][:], sc["inv2"][:], sc["inv2"][:]), ["inv2"], ["t1"])
            V(lambda e: e.tensor_mul(sc["va"][:], sc["va"][:], sc["t1"][:]), ["va", "t1"], ["va"])
            V(lambda e: e.tensor_scalar(sc["x"][:], sc["va"][:], 0.0, EPS, op0=ALU.max, op1=ALU.add), ["va"], ["x"])
            babylon_sqrt("f", 6 if kk == 0 else 4, "invf")
            V(lambda e: e.scalar_tensor_tensor(sc["bF"][:], sc["muf"][:], -1.0, sc["invf"][:],
                                               op0=ALU.mult, op1=ALU.mult), ["muf", "invf"], ["bF"])
            V(lambda e: e.tensor_mul(sc["t2"][:], sc["inv2"][:], sc["invf"][:]), ["inv2", "invf"], ["t2"])
            S(lambda e: e.activation(t3f[:], fcopy, AF.Identity,
                                     bias=sc["bF"][:], scale=sc["t2"][:]),
              ["fcop", "bF", "t2"], ["t3fbuf"])
            V((lambda par=par: (lambda e: e.tensor_add(x1b[par][:], x1b[par][:], t3f[:])))(),
              ["t3fbuf"], [f"x1b{par}"])
            SY((lambda par=par, kk=kk: (lambda e: e.dma_start(
                out=d_out[128 * kk:128 * (kk + 1), :], in_=x1b[par][:])))(),
               [f"x1b{par}"], ["outd"], sem="sO")


        for k in range(NP):
            pa_stats(k)
            pa_mms(k)
        for j in range(32):
            p1_w1(j)
        hi_mean(0, s_th1, s_q1, "th1", "q1", s_thpp, s_nthpp, s_q1pp, s_g1q, s_g1c,
                "thpp", "nthpp", "q1pp", "g1q")
        for j in range(32):
            p2_w1(j)
            p1_w2(j)
        hi_mean(1, s_th2, s_q2, "th2", "q2", s_thpp2, s_nthpp2, s_q2pp, s_g2q, s_g2c,
                "thpp2", "nthpp2", "q2pp", "g2q")
        for j in range(32):
            p2_w2(j)
        if DBG:
            V(lambda e: (
                e.tensor_copy(sdbg_th[0:1, 0:1], s_th1[:]),
                e.tensor_copy(sdbg_th[0:1, 1:2], s_q1[:]),
                e.tensor_copy(sdbg_th[0:1, 2:3], s_th2[:]),
                e.tensor_copy(sdbg_th[0:1, 3:4], s_q2[:]),
                e.tensor_copy(sdbg_th[0:1, 4:5], s_thpp[0:1, :]),
                e.tensor_copy(sdbg_th[0:1, 5:6], s_nthpp[0:1, :]))[-1],
              ["th1", "th2", "q1", "q2", "thpp", "nthpp"], ["dbgth"])
            SY(lambda e: e.dma_start(out=d_dbg_th, in_=sdbg_th[:]), ["dbgth"], ["dbgth"], sem="sO")
        for q in range(8):
            if q % 2 == 0:
                SY((lambda q=q: (lambda e: e.dma_start(
                    out=s_brow, in_=d_b1r[0:1, 512 * q:512 * (q + 2)])))(),
                   ["scr1"], ["scr0"])
            T((lambda q=q: (lambda e: [
                e.matmul(pF[0:1, 0:512], s_b1vcb[:, jj:jj + 1],
                         T1[:, jj * H + 512 * q: jj * H + 512 * (q + 1)],
                         start=(jj == 0), stop=(jj == 7)) for jj in range(8)][-1]))(),
              ["T1", "b1vcb"], ["pF"])
            V((lambda q=q: (lambda e: e.scalar_tensor_tensor(
                m1mov[0:1, 512 * q:512 * (q + 1)], pF[0:1, 0:512], s_q1[:],
                s_brow[0:1, 512 * (q % 2):512 * (q % 2 + 1)], op0=ALU.mult, op1=ALU.add)))(),
              ["pF", "q1", "scr0"], ["m1movb"])
        SY(lambda e: e.dma_start(out=s_brow, in_=d_b2r[0:1, :]), ["scr1"], ["scr0"])
        for q in range(2):
            T((lambda q=q: (lambda e: [
                e.matmul(pF[0:1, 0:512], s_b2vcb[:, cc:cc + 1],
                         T2[:, cc * D + 512 * q: cc * D + 512 * (q + 1)],
                         start=(cc == 0), stop=(cc == 31)) for cc in range(32)][-1]))(),
              ["T2", "b2vcb"], ["pF"])
            V((lambda q=q: (lambda e: e.scalar_tensor_tensor(
                m2mov[0:1, 512 * q:512 * (q + 1)], pF[0:1, 0:512], s_q2[:],
                s_brow[0:1, 512 * q:512 * (q + 1)], op0=ALU.mult, op1=ALU.add)))(),
              ["pF", "q2", "scr0"], ["m2movb"])

        V(lambda e: [e.tensor_scalar_mul(
            T1[:, jj * H:(jj + 1) * H], T1[:, jj * H:(jj + 1) * H], s_g1q[:, jj:jj + 1])
            for jj in range(8)][-1], ["g1q"], ["T1"])
        V(lambda e: [e.tensor_scalar_mul(
            T2[:, cc * D:(cc + 1) * D], T2[:, cc * D:(cc + 1) * D], s_g2q[:, cc:cc + 1])
            for cc in range(32)][-1], ["g2q"], ["T2"])
        for sl in range(8):
            T((lambda sl=sl: (lambda e: [e.matmul(
                pF[0:1, 0:512], c_ocol[:],
                T1[:, jj * H + 512 * sl: jj * H + 512 * (sl + 1)],
                start=(jj == 0), stop=(jj == 7)) for jj in range(8)][-1]))(),
              ["T1", "consts"], ["pF"])
            V(lambda e: e.tensor_copy(absb2[0:1, 0:512], pF[0:1, 0:512]), ["pF"], ["fcop"])
            SY((lambda sl=sl: (lambda e: e.dma_start(
                out=m1mov[1:2, 512 * sl:512 * (sl + 1)], in_=absb2[0:1, 0:512])))(),
               ["fcop"], ["m1movb"])
        for sl in range(2):
            T((lambda sl=sl: (lambda e: [e.matmul(
                pF[0:1, 0:512], c_ocol[:],
                T2[:, cc * D + 512 * sl: cc * D + 512 * (sl + 1)],
                start=(cc == 0), stop=(cc == 31)) for cc in range(32)][-1]))(),
              ["T2", "consts"], ["pF"])
            V(lambda e: e.tensor_copy(absb2[0:1, 0:512], pF[0:1, 0:512]), ["pF"], ["fcop"])
            SY((lambda sl=sl: (lambda e: e.dma_start(
                out=m2mov[1:2, 512 * sl:512 * (sl + 1)], in_=absb2[0:1, 0:512])))(),
               ["fcop"], ["m2movb"])
        if DBG:
            SY(lambda e: e.dma_start(out=d_dbg_t1, in_=T1[:, 0:1024]), ["T1"], ["dbg1"], sem="sO")
            SY(lambda e: e.dma_start(out=d_dbg_bt, in_=m1mov[0:1, :]), ["m1movb"], ["dbg2"], sem="sO")
            SY(lambda e: e.dma_start(out=d_dbg_b2, in_=m2mov[0:1, :]), ["m2movb"], ["dbg2b"], sem="sO")
            SY(lambda e: e.dma_start(out=d_dbg_t2, in_=T2[:, 0:1024]), ["T2"], ["dbg2c"], sem="sO")

        pa_stats(NP)
        pa_stats(NP + 1)
        pa_mms(NP)
        pa_post(NP)
        x1prep(NP)
        for k in range(NP, NP + NT):
            if k + 2 < NP + NT:
                pa_stats(k + 2)
            if k + 1 < NP + NT:
                pa_mms(k + 1)
            ffn_heavy(k)
            if k + 1 < NP + NT:
                pa_post(k + 1)
                x1prep(k + 1)
            ffn_tail(k)

        with nc.Block() as block:
            sems = {n: stack.enter_context(nc.semaphore(n)) for n in sems_names}

            @block.sync
            def _(sync):
                em.replay("SY", sync, sems)
                sync.wait_ge(sems["sO"], em.cnt["sO"])

            @block.tensor
            def _(tensor):
                em.replay("T", tensor, sems)

            @block.scalar
            def _(scalar):
                em.replay("S", scalar, sems)

            @block.vector
            def _(vector):
                em.replay("V", vector, sems)

            @block.gpsimd
            def _(gpsimd):
                em.replay("GP", gpsimd, sems)

    return nc


_NC_CACHE = None
LAST_RESULTS = None


def _get_nc():
    global _NC_CACHE
    if _NC_CACHE is None:
        _NC_CACHE = build_program()
    return _NC_CACHE


def kernel(L0_attn_x, L0_ffn_ln1_g, L0_ffn_ln1_b, L0_ffn_w1, L0_ffn_b1,
           L0_ffn_ln2_g, L0_ffn_ln2_b, L0_ffn_w2, L0_ffn_b2):
    x = np.asarray(L0_attn_x, dtype=np.float32)
    w1 = np.asarray(L0_ffn_w1, dtype=np.float32)
    w2 = np.asarray(L0_ffn_w2, dtype=np.float32)
    bf = ml_dtypes.bfloat16

    w1t = np.ascontiguousarray(w1.T)
    w2t = np.ascontiguousarray(w2.T)
    g1c = np.ascontiguousarray(np.asarray(L0_ffn_ln1_g, np.float32).reshape(8, 128).T)
    b1vc = np.ascontiguousarray(np.asarray(L0_ffn_ln1_b, np.float32).reshape(8, 128).T)
    g2c = np.ascontiguousarray(np.asarray(L0_ffn_ln2_g, np.float32).reshape(32, 128).T)
    b2vc = np.ascontiguousarray(np.asarray(L0_ffn_ln2_b, np.float32).reshape(32, 128).T)
    b1r = np.asarray(L0_ffn_b1, np.float32).reshape(1, H)
    b2r = np.asarray(L0_ffn_b2, np.float32).reshape(1, D)

    tri = np.tril(np.ones((128, 128), np.float32))
    consts = dict(
        triu=np.ascontiguousarray(tri.T).astype(bf),
        stril=np.ascontiguousarray(np.tril(np.ones((128, 128), np.float32), -1)).astype(bf),
        ones128=np.ones((128, 128), bf),
        ident=np.eye(128, dtype=bf),
        onesrow=np.ones((1, 128), bf),
        onesrowf=np.ones((1, 128), np.float32),
        identf=np.eye(128, dtype=np.float32),
        onescol=np.ones((128, 1), bf),
        w1t=w1t, w2t=w2t, g1c=g1c, b1vc=b1vc, g2c=g2c, b2vc=b2vc,
        b1r=b1r, b2r=b2r,
    )

    in_maps = []
    for c in range(8):
        b, hhalf = divmod(c, 2)
        xm = np.ascontiguousarray(x[b, 2048 * hhalf: 2048 * (hhalf + 1)])
        xp = np.ascontiguousarray(x[b, 0:2048]) if hhalf else np.zeros((2048, D), np.float32)
        tglob = 2048 * hhalf + (np.arange(NT)[None, :] * 128 + np.arange(128)[:, None])
        counts = (tglob + 1).astype(np.float32)
        aseed = (1.0 / np.sqrt(1.0 / counts + EPS)).astype(np.float32)
        in_maps.append(dict(xm=xm, xp=xp, counts=counts, aseed=aseed, **consts))

    nc = _get_nc()
    res = run_bass_kernel_spmd(nc, in_maps, core_ids=list(range(8)))
    global LAST_RESULTS
    LAST_RESULTS = res.results
    out = np.empty((4, 4096, D), dtype=np.float32)
    for c in range(8):
        b, hhalf = divmod(c, 2)
        out[b, 2048 * hhalf: 2048 * (hhalf + 1)] = res.results[c]["out"]
    return out


# version 22
# speedup vs baseline: 1.3005x; 1.3005x over previous
import os
import numpy as np
import ml_dtypes
import concourse.bass as bass
import concourse.mybir as mybir
from concourse.bass_utils import run_bass_kernel_spmd
from contextlib import ExitStack
from collections import defaultdict

F32 = mybir.dt.float32
BF16 = mybir.dt.bfloat16
AF = mybir.ActivationFunctionType
ALU = mybir.AluOpType
AX = mybir.AxisListType

DBG = os.environ.get("KDBG", "0") == "1"
EPS = 1e-5
D = 1024
H = 4096
NT = 16
NP = 16


class Buf:
    __slots__ = ("w", "r")
    def __init__(self):
        self.w = None
        self.r = []


class Em:
    ENGS = ("T", "S", "V", "SY", "GP")
    SEM = {"T": "sT", "S": "sS", "V": "sV", "SY": "sDX", "GP": "sDW"}

    def __init__(self):
        self.q = {e: [] for e in self.ENGS}
        self.cnt = {"sT": 0, "sS": 0, "sV": 0, "sDX": 0, "sDW": 0, "sO": 0, "sGP": 0, "sDS": 0}
        self.water = {e: {} for e in self.ENGS}
        self.bufs = defaultdict(Buf)

    def op(self, eng, builder, reads=(), writes=(), sem=None):
        own = sem or self.SEM[eng]
        step = 16 if own in ("sDX", "sDW", "sO", "sDS") else 1
        deps = []
        for n in reads:
            b = self.bufs[n]
            if b.w is not None:
                deps.append(b.w)
        for n in writes:
            b = self.bufs[n]
            deps.extend(b.r)
            if b.w is not None:
                deps.append(b.w)
        waits = {}
        self_edge = eng in ("V", "S", "GP")
        for (s, v) in deps:
            if s == own and not self_edge:
                continue
            if self.water[eng].get(s, 0) >= v:
                continue
            waits[s] = max(waits.get(s, 0), v)
        for s, v in waits.items():
            self.water[eng][s] = v
        self.cnt[own] += step
        tok = (own, self.cnt[own])
        self.q[eng].append((waits, builder, own, step))
        for n in reads:
            self.bufs[n].r.append(tok)
        for n in writes:
            b = self.bufs[n]
            b.w = tok
            b.r = []
        return tok

    def replay(self, eng_name, eng, sems):
        for waits, builder, own, step in self.q[eng_name]:
            for s, v in waits.items():
                eng.wait_ge(sems[s], v)
            builder(eng).then_inc(sems[own], step)


def build_program():
    nc = bass.Bass("TRN2", target_bir_lowering=False, debug=False)

    dt = lambda n, s, ty, k: nc.dram_tensor(n, s, ty, kind=k).ap()
    d_xm = dt("xm", [2048, D], F32, "ExternalInput")
    d_xp = dt("xp", [2048, D], F32, "ExternalInput")
    d_counts = dt("counts", [128, NT], F32, "ExternalInput")
    d_aseed = dt("aseed", [128, NT], F32, "ExternalInput")
    d_w1t = dt("w1t", [D, H], F32, "ExternalInput")
    d_w2t = dt("w2t", [H, D], F32, "ExternalInput")
    d_g1c = dt("g1c", [128, 8], F32, "ExternalInput")
    d_b1vc = dt("b1vc", [128, 8], F32, "ExternalInput")
    d_g2c = dt("g2c", [128, 32], F32, "ExternalInput")
    d_b2vc = dt("b2vc", [128, 32], F32, "ExternalInput")
    d_b1r = dt("b1r", [1, H], F32, "ExternalInput")
    d_b2r = dt("b2r", [1, D], F32, "ExternalInput")
    d_triu = dt("triu", [128, 128], BF16, "ExternalInput")
    d_stril = dt("stril", [128, 128], BF16, "ExternalInput")
    d_ones = dt("ones128", [128, 128], BF16, "ExternalInput")
    d_id = dt("ident", [128, 128], BF16, "ExternalInput")
    d_orow = dt("onesrow", [1, 128], BF16, "ExternalInput")
    d_orowf = dt("onesrowf", [1, 128], F32, "ExternalInput")
    d_idf = dt("identf", [128, 128], F32, "ExternalInput")
    d_ocol = dt("onescol", [128, 1], BF16, "ExternalInput")
    d_out = dt("out", [2048, D], F32, "ExternalOutput")
    if DBG:
        d_dbg_u = dt("dbg_u", [128, D], BF16, "ExternalOutput")
        d_dbg_pa = dt("dbg_pa", [128, D], F32, "ExternalOutput")
        d_dbg_sc = dt("dbg_sc", [128, 16], F32, "ExternalOutput")
        d_dbg_t1 = dt("dbg_t1", [128, 1024], BF16, "ExternalOutput")
        d_dbg_x1 = dt("dbg_x1", [128, D], F32, "ExternalOutput")
        d_dbg_g = dt("dbg_g", [128, H], BF16, "ExternalOutput")
        d_dbg_bt = dt("dbg_bt", [1, H], BF16, "ExternalOutput")
        d_dbg_b2 = dt("dbg_b2", [1, D], BF16, "ExternalOutput")
        d_dbg_t2 = dt("dbg_t2", [128, 1024], BF16, "ExternalOutput")
        d_dbg_th = dt("dbg_th", [1, 8], F32, "ExternalOutput")
        d_dbg_aux = dt("dbg_aux", [2, 128], BF16, "ExternalOutput")
        d_dbg_m1 = dt("dbg_m1", [2, H], BF16, "ExternalOutput")

    em = Em()

    with ExitStack() as stack:
        def sb(name, shape, dtype=F32):
            return stack.enter_context(nc.sbuf_tensor(name, shape, dtype))

        T1 = sb("T1", [128, 8 * H], BF16)
        T2 = sb("T2", [128, 32 * D], BF16)
        scr = sb("scr", [128, 2048], F32)
        x1cT = scr[:, 0:512].bitcast(BF16)

        x0b = [sb(f"x0b{i}", [128, D], F32) for i in range(2)]
        x1b = [sb(f"x1b{i}", [128, D], F32) for i in range(2)]
        ub = [sb(f"ub{i}", [128, D], BF16) for i in range(3)]
        x1c = [sb(f"x1c{i}", [128, D], BF16) for i in range(2)]
        sqjunk = scr[:, :].bitcast(BF16)
        alph = [sb(f"alph{i}", [128, 1], F32) for i in range(2)]
        auxc = [sb(f"auxc{i}", [128, 2], F32) for i in range(2)]
        auxc2 = sb("auxc2", [128, 2], F32)
        auxL1 = [sb(f"auxL1{i}", [2, 128], BF16) for i in range(2)]
        auxL2 = sb("auxL2", [2, 128], BF16)
        m1mov = sb("m1mov", [2, H], BF16)
        m2mov = sb("m2mov", [2, D], BF16)
        gb = sb("gb", [128, H], BF16)
        bfs = sb("bfs", [128, 2048], BF16)
        absb = bfs[:, 0:1024]
        acopy = bfs[:, 0:1024]
        absb2 = bfs[:, 1024:1536]
        fcopy = bfs[:, 1024:2048]
        zT = sb("zT", [128, H], BF16)
        t3 = sb("t3", [128, D], F32)
        s_brow = t3[0:1, 0:1024]
        t3f = sb("t3f", [128, D], F32)
        c_triu = sb("c_triu", [128, 128], BF16)
        c_stril = sb("c_stril", [128, 128], BF16)
        c_ones = sb("c_ones", [128, 128], BF16)
        c_id = sb("c_id", [128, 128], BF16)
        c_orow = sb("c_orow", [1, 128], BF16)
        c_orowf = sb("c_orowf", [1, 128], F32)
        c_idf = sb("c_idf", [128, 128], F32)
        s_lv = [sb(f"s_lv{i}", [128, 32], F32) for i in range(2)]
        s_lvr = sb("s_lvr", [128, 32], F32)
        s_acc = sb("s_acc", [128, 1], F32)
        s_red = sb("s_red", [1, 128], F32)
        s_Lt = sb("s_Lt", [1, 1], F32)
        s_Lpp = sb("s_Lpp", [128, 1], F32)
        s_R = sb("s_R", [1, 1], F32)
        c_ocol = sb("c_ocol", [128, 1], BF16)
        s_counts = sb("s_counts", [128, NT], F32)
        s_aseed = sb("s_aseed", [128, NT], F32)
        s_rs = sb("s_rs", [128, NT], F32)
        s_rs2 = sb("s_rs2", [128, NT], F32)
        s_g1c = sb("s_g1c", [128, 8], F32)
        s_b1vc = sb("s_b1vc", [128, 8], F32)
        s_b1vcb = sb("s_b1vcb", [128, 8], BF16)
        s_g2c = sb("s_g2c", [128, 32], F32)
        s_b2vc = sb("s_b2vc", [128, 32], F32)
        s_b2vcb = sb("s_b2vcb", [128, 32], BF16)
        s_g1q = sb("s_g1q", [128, 8], F32)
        s_g2q = sb("s_g2q", [128, 32], F32)
        s_bn = sb("s_bn", [128, 12], F32)
        s_bnA = sb("s_bnA", [128, 12], F32)
        gsc = {k: sb(f"gsc_{k}", [128, 1], F32) for k in
               ("ms", "M2s", "msqs", "va", "x", "t1", "t2",
                "mu", "mua", "inv0", "nb0", "inva", "sA", "bA")}
        s_gmsq = sb("s_gmsq", [128, 4], F32)
        gaccq = [sb(f"gaccq{i}", [128, 1], F32) for i in range(8)]
        s_gsq = sb("s_gsq", [128, 1], F32)
        sc = {k: sb(f"sc_{k}", [128, 1], F32) for k in
              ("ms", "mu", "M2s", "msqs", "va", "vb", "x", "t1", "t2", "t3",
               "t4", "inv0", "nb0", "inva", "sA", "bA", "inv1",
               "inv2", "invf", "bF", "mua", "mu2", "muf", "mu1")}
        s_msq = sb("s_msq", [128, 4], F32)
        bab = {k: sb(f"bab_{k}", [128, 1], F32) for k in
               ("pre", "preV", "x0", "a", "x1", "g", "f")}
        s_th1 = sb("s_th1", [1, 1], F32)
        s_q1 = sb("s_q1", [1, 1], F32)
        s_th2 = sb("s_th2", [1, 1], F32)
        s_q2 = sb("s_q2", [1, 1], F32)
        s_m1 = sb("s_m1", [1, 1], F32)
        s_m2 = sb("s_m2", [1, 1], F32)
        s_thpp = sb("s_thpp", [128, 1], F32)
        s_nthpp = sb("s_nthpp", [128, 1], F32)
        s_thpp2 = sb("s_thpp2", [128, 1], F32)
        s_nthpp2 = sb("s_nthpp2", [128, 1], F32)
        s_q1pp = sb("s_q1pp", [128, 1], F32)
        s_q2pp = sb("s_q2pp", [128, 1], F32)
        if DBG:
            sdbg_sc = sb("sdbg_sc", [128, 16], F32)
            sdbg_th = sb("sdbg_th", [1, 8], F32)

        pA = stack.enter_context(nc.psum_tensor("pA", [128, 1024], F32))
        ptr = stack.enter_context(nc.psum_tensor("ptr", [128, 1024], BF16))
        pv = [stack.enter_context(nc.psum_tensor(f"pv{i}", [128, 512], F32))
              for i in range(2)]
        pF = stack.enter_context(nc.psum_tensor("pF", [128, 1024], F32))
        pX = stack.enter_context(nc.psum_tensor("pX", [128, 512], F32))

        sems_names = ("sT", "sS", "sV", "sDX", "sDW", "sO", "sGP", "sDS")

        V = lambda b, r, w: em.op("V", b, r, w)
        S = lambda b, r, w: em.op("S", b, r, w)
        T = lambda b, r, w: em.op("T", b, r, w)
        SY = lambda b, r, w, sem="sDX": em.op("SY", b, r, w, sem=sem)
        GP = lambda b, r, w: em.op("GP", b, r, w)
        GC = lambda b, r, w: em.op("GP", b, r, w, sem="sGP")
        SD = lambda b, r, w: em.op("S", b, r, w, sem="sDS")

        def dma(dstap, srcap):
            return lambda e: e.dma_start(out=dstap, in_=srcap)

        for dst, src in ((c_triu[:], d_triu), (c_stril[:], d_stril),
                         (c_ones[:], d_ones), (c_id[:], d_id),
                         (c_orow[:], d_orow), (c_orowf[:], d_orowf),
                         (c_idf[:], d_idf),
                         (c_ocol[:], d_ocol), (s_counts[:], d_counts),
                         (s_aseed[:], d_aseed),
                         (s_g1c[:], d_g1c), (s_b1vc[:], d_b1vc),
                         (s_g2c[:], d_g2c), (s_b2vc[:], d_b2vc)):
            SY(dma(dst, src), [], ["consts"])

        V(lambda e: e.reciprocal(s_rs[:], s_counts[:]), ["consts"], ["rs"])
        V(lambda e: e.tensor_mul(s_rs2[:], s_rs[:], s_rs[:]), ["rs"], ["rs2"])
        V(lambda e: e.tensor_copy(s_b1vcb[:], s_b1vc[:]), ["consts"], ["b1vcb"])
        V(lambda e: e.tensor_copy(s_b2vcb[:], s_b2vc[:]), ["consts"], ["b2vcb"])
        V(lambda e: [e.memset(bab[kk][:], 0.7 if kk == "x1" else 1.0)
                     for kk in bab][-1], [], [f"bab_{kk2}" for kk2 in
                     ("pre", "preV", "x0", "a", "x1", "g", "f")])

        wstg = [(scr[:, 0:1024], ["scr0L", "scr0R"]),
                (scr[:, 1024:2048], ["scr1L", "scr1R"]),
                (gb[:, 0:2048].bitcast(F32), ["gbs2"]),
                (gb[:, 2048:4096].bitcast(F32), ["gbs3"]),
                (zT[:, 0:2048].bitcast(F32), ["zbs4"]),
                (zT[:, 2048:4096].bitcast(F32), ["zbs5"])]

        def w1src(j):
            dj, qq = divmod(j, 4)
            return d_w1t[128 * dj:128 * (dj + 1), 1024 * qq:1024 * (qq + 1)]

        def w2src(j):
            return d_w2t[128 * j:128 * (j + 1), :]

        def wdma(i, srcap):
            buf, names = wstg[i % 6]
            Q = SY if i % 2 == 0 else SD
            Q(dma(buf, srcap), [], names)

        def wabs(i, which, j):
            buf, names = wstg[i % 6]
            S((lambda s=buf, j=j, w=which: (lambda e: e.activation(
                absb[:], s[:], AF.Abs, accum_out=s_lv[w][:, j:j + 1])))(),
              names, ["absb", f"lv{which}"])

        def wtern1(i, j):
            buf, names = wstg[i % 6]
            dj, qq = divmod(j, 4)
            dstsl = T1[:, dj * H + 1024 * qq: dj * H + 1024 * (qq + 1)]
            V((lambda s=buf: (lambda e: e.tensor_scalar(
                absb[:], s[:], s_thpp[:], 1.0, op0=ALU.is_ge, op1=ALU.subtract)))(),
              names + ["thpp"], ["absb"])
            V((lambda s=buf, dst2=dstsl: (lambda e: e.scalar_tensor_tensor(
                dst2, s[:], s_nthpp[:], absb[:], op0=ALU.is_ge, op1=ALU.add)))(),
              names + ["nthpp", "absb"], ["T1"])

        def wtern2(j):
            buf, names = wstg[j % 2]
            dstsl = T2[:, j * D:(j + 1) * D]
            V((lambda s=buf: (lambda e: e.tensor_scalar(
                absb[:], s[:], s_thpp2[:], 1.0, op0=ALU.is_ge, op1=ALU.subtract)))(),
              names + ["thpp2"], ["absb"])
            V((lambda s=buf, dst2=dstsl: (lambda e: e.scalar_tensor_tensor(
                dst2, s[:], s_nthpp2[:], absb[:], op0=ALU.is_ge, op1=ALU.add)))(),
              names + ["nthpp2", "absb"], ["T2"])

        NW = float(H * D)
        def hi_mean(i, th_t, q_t, thk, qk, thpp_t, nthpp_t, qpp_t, gq_t, gc_t,
                    thppk, nthppk, qppk, gqk):
            lvk = f"lv{i}"
            V((lambda i=i: (lambda e: e.tensor_reduce(s_acc[:], s_lv[i][:], AX.X, ALU.add)))(),
              [lvk, "absb"], ["hacc"])
            T(lambda e: e.matmul(pF[0:1, 0:128], s_acc[:], c_idf[:], start=True, stop=True),
              ["hacc", "consts"], ["pF"])
            V(lambda e: e.tensor_copy(s_red[:], pF[0:1, 0:128]), ["pF"], ["red"])
            for wl in (64, 32, 16, 8, 4, 2, 1):
                V((lambda wl=wl: (lambda e: e.tensor_add(
                    s_red[0:1, 0:wl], s_red[0:1, 0:wl], s_red[0:1, wl:2 * wl])))(),
                  ["red"], ["red"])
            V(lambda e: e.tensor_scalar_mul(s_Lt[:], s_red[0:1, 0:1], 1.0 / 4096.0), ["red"], ["Lt"])
            T(lambda e: e.matmul(pX[:, 0:1], c_orowf[:], s_Lt[:], start=True, stop=True),
              ["Lt", "consts"], ["pX"])
            V(lambda e: e.tensor_copy(s_Lpp[:], pX[:, 0:1]), ["pX"], ["Lpp"])
            V((lambda i=i: (lambda e: e.tensor_scalar(
                s_lvr[:], s_lv[i][:], s_Lpp[:], None, op0=ALU.subtract)))(),
              [lvk, "Lpp"], ["lvr"])
            V(lambda e: e.tensor_reduce(s_acc[:], s_lvr[:], AX.X, ALU.add), ["lvr"], ["hacc"])
            T(lambda e: e.matmul(pF[0:1, 0:128], s_acc[:], c_idf[:], start=True, stop=True),
              ["hacc", "consts"], ["pF"])
            V(lambda e: e.tensor_copy(s_red[:], pF[0:1, 0:128]), ["pF"], ["red"])
            for wl in (64, 32, 16, 8, 4, 2, 1):
                V((lambda wl=wl: (lambda e: e.tensor_add(
                    s_red[0:1, 0:wl], s_red[0:1, 0:wl], s_red[0:1, wl:2 * wl])))(),
                  ["red"], ["red"])
            V(lambda e: e.tensor_scalar_mul(s_m1[:], s_Lt[:], 2.0 ** -10), ["Lt"], ["m1"])
            V(lambda e: e.tensor_scalar_mul(s_m2[:], s_red[0:1, 0:1], 2.0 ** -22), ["red"], ["m2"])
            V(lambda e: e.tensor_add(s_m1[:], s_m1[:], s_m2[:]), ["m1", "m2"], ["m1"])
            V(lambda e: e.tensor_scalar_max(s_m2[:], s_m1[:], 1e-5), ["m1"], ["m2"])
            V((lambda t=th_t: (lambda e: e.reciprocal(t[:], s_m2[:])))(), ["m2"], [thk])
            V((lambda t=th_t, u=q_t: (lambda e: e.reciprocal(u[:], t[:])))(), [thk], [qk])
            V((lambda t=th_t, u=q_t: (lambda e: e.tensor_scalar_mul(t[:], u[:], 0.5)))(), [qk], [thk])
            T((lambda t=th_t: (lambda e: e.matmul(pX[:, 1:2], c_orowf[:], t[:], start=True, stop=True)))(),
              [thk, "consts"], ["pX"])
            V((lambda t=thpp_t: (lambda e: e.tensor_copy(t[:], pX[:, 1:2])))(), ["pX"], [thppk])
            V((lambda t=nthpp_t: (lambda e: e.tensor_scalar_mul(t[:], pX[:, 1:2], -1.0)))(), ["pX"], [nthppk])
            V((lambda t=qpp_t, u=thpp_t: (lambda e: e.tensor_scalar_mul(t[:], u[:], 2.0)))(), [thppk], [qppk])
            V((lambda t=gq_t, u=gc_t, v2=qpp_t: (lambda e: e.tensor_scalar_mul(t[:], u[:], v2[:])))(),
              ["consts", qppk], [gqk])

        def combine1024(mu_key, rs_slice=None, rs2_slice=None, E=None, scd=None, P="", msqt=None, bnk="bn", bnt=None):
            useV = E is None or E is V
            if E is None:
                E = V
            if scd is None:
                scd = sc
            if msqt is None:
                msqt = s_msq
            if bnt is None:
                bnt = s_bn
            mu = scd[mu_key]
            if useV:
                E(lambda e: e.tensor_reduce(scd["ms"][:], bnt[:, 1::3], AX.X, ALU.add), [bnk], [P + "ms"])
                E(lambda e: e.tensor_mul(msqt[:], bnt[:, 1::3], bnt[:, 1::3]), [bnk], [P + "msq"])
                E(lambda e: e.tensor_reduce(scd["msqs"][:], msqt[:], AX.X, ALU.add), [P + "msq"], [P + "msqs"])
                E(lambda e: e.tensor_reduce(scd["M2s"][:], bnt[:, 2::3], AX.X, ALU.add), [bnk], [P + "M2s"])
            else:
                E(lambda e: e.tensor_add(msqt[:, 0:1], bnt[:, 1:2], bnt[:, 4:5]), [bnk], [P + "msq"])
                E(lambda e: e.tensor_add(msqt[:, 1:2], bnt[:, 7:8], bnt[:, 10:11]), [bnk], [P + "msq"])
                E(lambda e: e.tensor_add(scd["ms"][:], msqt[:, 0:1], msqt[:, 1:2]), [P + "msq"], [P + "ms"])
                E(lambda e: e.tensor_mul(msqt[:, 0:1], bnt[:, 1:2], bnt[:, 1:2]), [bnk], [P + "msq"])
                E(lambda e: e.tensor_mul(msqt[:, 1:2], bnt[:, 4:5], bnt[:, 4:5]), [bnk], [P + "msq"])
                E(lambda e: e.tensor_mul(msqt[:, 2:3], bnt[:, 7:8], bnt[:, 7:8]), [bnk], [P + "msq"])
                E(lambda e: e.tensor_mul(msqt[:, 3:4], bnt[:, 10:11], bnt[:, 10:11]), [bnk], [P + "msq"])
                E(lambda e: e.tensor_add(msqt[:, 0:1], msqt[:, 0:1], msqt[:, 1:2]), [P + "msq"], [P + "msq"])
                E(lambda e: e.tensor_add(msqt[:, 2:3], msqt[:, 2:3], msqt[:, 3:4]), [P + "msq"], [P + "msq"])
                E(lambda e: e.tensor_add(scd["msqs"][:], msqt[:, 0:1], msqt[:, 2:3]), [P + "msq"], [P + "msqs"])
                E(lambda e: e.tensor_add(msqt[:, 0:1], bnt[:, 2:3], bnt[:, 5:6]), [bnk], [P + "msq"])
                E(lambda e: e.tensor_add(msqt[:, 1:2], bnt[:, 8:9], bnt[:, 11:12]), [bnk], [P + "msq"])
                E(lambda e: e.tensor_add(scd["M2s"][:], msqt[:, 0:1], msqt[:, 1:2]), [P + "msq"], [P + "M2s"])
            E(lambda e: e.tensor_scalar_mul(mu[:], scd["ms"][:], 0.25), [P + "ms"], [P + mu_key])
            E(lambda e: e.tensor_mul(scd["t1"][:], mu[:], mu[:]), [P + mu_key], [P + "t1"])
            E(lambda e: e.tensor_scalar_mul(scd["va"][:], scd["M2s"][:], 1.0 / 1024.0), [P + "M2s"], [P + "va"])
            if useV:
                E(lambda e: e.scalar_tensor_tensor(scd["va"][:], scd["msqs"][:], 0.25, scd["va"][:],
                                                   op0=ALU.mult, op1=ALU.add), [P + "msqs", P + "va"], [P + "va"])
            else:
                E(lambda e: e.tensor_scalar_mul(scd["t2"][:], scd["msqs"][:], 0.25), [P + "msqs"], [P + "t2"])
                E(lambda e: e.tensor_add(scd["va"][:], scd["va"][:], scd["t2"][:]), [P + "va", P + "t2"], [P + "va"])
            if rs_slice is not None:
                E((lambda r2=rs_slice: (lambda e: e.tensor_mul(mu[:], mu[:], r2)))(),
                  [P + mu_key, "rs"], [P + mu_key])
                E(lambda e: e.tensor_sub(scd["va"][:], scd["va"][:], scd["t1"][:]), [P + "va", P + "t1"], [P + "va"])
                E((lambda r2=rs2_slice: (lambda e: e.tensor_mul(scd["va"][:], scd["va"][:], r2)))(),
                  [P + "va", "rs2"], [P + "va"])
                E(lambda e: e.tensor_scalar(scd["x"][:], scd["va"][:], 0.0, EPS, op0=ALU.max, op1=ALU.add), [P + "va"], [P + "x"])
            else:
                E(lambda e: e.tensor_sub(scd["va"][:], scd["va"][:], scd["t1"][:]), [P + "va", P + "t1"], [P + "va"])
                E(lambda e: e.tensor_scalar(scd["x"][:], scd["va"][:], 0.0, EPS, op0=ALU.max, op1=ALU.add), [P + "va"], [P + "x"])
        def babylon(state_key, iters, inv_key, seed=None, E=None, scd=None, P=""):
            if E is None:
                E = V
            if scd is None:
                scd = sc
            st = bab[state_key]
            bk = "bab_" + state_key
            if seed is not None:
                E((lambda s2=seed: (lambda e: e.tensor_copy(st[:], s2)))(),
                  ["consts"], [bk])
            for _ in range(iters):
                E(lambda e: e.tensor_mul(scd["t2"][:], st[:], st[:]), [bk], [P + "t2"])
                E(lambda e: e.tensor_mul(scd["t2"][:], scd["x"][:], scd["t2"][:]),
                  [P + "x", P + "t2"], [P + "t2"])
                E(lambda e: e.tensor_scalar(scd["t2"][:], scd["t2"][:], -0.5, 1.5,
                                            op0=ALU.mult, op1=ALU.add), [P + "t2"], [P + "t2"])
                E(lambda e: e.tensor_mul(st[:], st[:], scd["t2"][:]), [bk, P + "t2"], [bk])
            E((lambda t=scd[inv_key]: (lambda e: e.tensor_copy(t[:], st[:])))(), [bk], [P + inv_key])
        def babylon_sqrt(state_key, iters, inv_key):
            st = bab[state_key]
            bk = "bab_" + state_key
            for _ in range(iters):
                V(lambda e: e.reciprocal(sc["t2"][:], st[:]), [bk], ["t2"])
                V(lambda e: e.tensor_mul(sc["t2"][:], sc["x"][:], sc["t2"][:]), ["x", "t2"], ["t2"])
                V(lambda e: e.tensor_add(sc["t2"][:], st[:], sc["t2"][:]), [bk, "t2"], ["t2"])
                V(lambda e: e.tensor_scalar_mul(st[:], sc["t2"][:], 0.5), ["t2"], [bk])
            V((lambda t=sc[inv_key]: (lambda e: e.reciprocal(t[:], st[:])))(), [bk], [inv_key])

        def pa_stats(k):
            main = k >= NP
            kk = k - NP
            par = k % 2
            xsrc = d_xm[128 * kk:128 * (kk + 1), :] if main else d_xp[128 * k:128 * (k + 1), :]
            xk = f"x0b{par}"
            SY((lambda dst2=x0b[par][:], s2=xsrc: (lambda e: e.dma_start(out=dst2, in_=s2)))(),
               [], [xk])
            V((lambda par=par: (lambda e: e.bn_stats(s_bnA[:, 0:6], x0b[par][:, 0:512])))(),
              [xk], ["bnA"])
            V((lambda par=par: (lambda e: e.bn_stats(s_bnA[:, 6:12], x0b[par][:, 512:1024])))(),
              [xk], ["bnA"])
            combine1024("mu", E=GC, scd=gsc, P="g", msqt=s_gmsq, bnk="bnA", bnt=s_bnA)
            babylon("pre" if not main else "x0", 6 if k == 0 or kk == 0 else 3, "inv0",
                    E=GC, scd=gsc, P="g")
            GC(lambda e: e.tensor_scalar_mul(gsc["t1"][:], gsc["mu"][:], -1.0), ["gmu"], ["gt1"])
            GC(lambda e: e.tensor_mul(gsc["nb0"][:], gsc["t1"][:], gsc["inv0"][:]), ["gt1", "ginv0"], ["gnb0"])
            u3 = k % 3
            S((lambda par=par, u3=u3: (lambda e: e.activation(
                ub[u3][:], x0b[par][:], AF.Identity, bias=gsc["nb0"][:], scale=gsc["inv0"][:])))(),
              [xk, "gnb0", "ginv0"], [f"ub{u3}"])

        def pa_stats_pre(k):
            par = k % 2
            xk = f"x0b{par}"
            SY((lambda dst2=x0b[par][:], s2=d_xp[128 * k:128 * (k + 1), :]:
                (lambda e: e.dma_start(out=dst2, in_=s2)))(), [], [xk])
            if k % 2 == 0:
                E, scd, P, msqt, bnk, bnt, st = GC, gsc, "g", s_gmsq, "bnA", s_bnA, "pre"
            else:
                E, scd, P, msqt, bnk, bnt, st = V, sc, "", s_msq, "bn", s_bn, "preV"
            V((lambda par=par, bnt=bnt: (lambda e: e.bn_stats(bnt[:, 0:6], x0b[par][:, 0:512])))(),
              [xk], [bnk])
            V((lambda par=par, bnt=bnt: (lambda e: e.bn_stats(bnt[:, 6:12], x0b[par][:, 512:1024])))(),
              [xk], [bnk])
            combine1024("mu", E=E, scd=scd, P=P, msqt=msqt, bnk=bnk, bnt=bnt)
            babylon(st, 6 if k <= 1 else 3, "inv0", E=E, scd=scd, P=P)
            E((lambda scd=scd: (lambda e: e.tensor_scalar_mul(scd["t1"][:], scd["mu"][:], -1.0)))(),
              [P + "mu"], [P + "t1"])
            E((lambda scd=scd: (lambda e: e.tensor_mul(scd["nb0"][:], scd["t1"][:], scd["inv0"][:])))(),
              [P + "t1", P + "inv0"], [P + "nb0"])
            u3 = k % 3
            E((lambda par=par, u3=u3, scd=scd: (lambda e: e.tensor_scalar(
                ub[u3][:], x0b[par][:], scd["inv0"][:], scd["nb0"][:],
                op0=ALU.mult, op1=ALU.add)))(),
              [xk, P + "nb0", P + "inv0"], [f"ub{u3}"])

        def pa_mms(k):
            main = k >= NP
            kk = k - NP
            u3 = k % 3
            p3 = (k - 1) % 3
            if not main:
                T((lambda u3=u3, first=(k == 0): (lambda e: (
                    e.matmul(pA[:, 0:512], c_ones[:], ub[u3][:, 0:512], start=first, stop=False),
                    e.matmul(pA[:, 512:1024], c_ones[:], ub[u3][:, 512:1024], start=first, stop=False))[-1]))(),
                  [f"ub{u3}", "consts"], ["pA"])
            else:
                T((lambda u3=u3, p3=p3, prom=(kk > 0): (lambda e: (
                    ([e.matmul(pA[:, 0:512], c_stril[:], ub[p3][:, 0:512], start=False, stop=False),
                      e.matmul(pA[:, 512:1024], c_stril[:], ub[p3][:, 512:1024], start=False, stop=False)]
                     if prom else []),
                    e.matmul(pA[:, 0:512], c_triu[:], ub[u3][:, 0:512], start=False, stop=False),
                    e.matmul(pA[:, 512:1024], c_triu[:], ub[u3][:, 512:1024], start=False, stop=False))[-1]))(),
                  [f"ub{u3}", f"ub{p3}", "consts"], ["pA"])

        def pa_post(k):
            kk = k - NP
            par = k % 2
            xk = f"x0b{par}"
            S((lambda kk=kk: (lambda e: e.activation(
                acopy, pA[:], AF.Identity, scale=s_rs[:, kk:kk + 1])))(),
              ["pA", "rs"], ["absb"])
            V(lambda e: e.bn_stats(s_bnA[:, 0:6], acopy[:, 0:512]), ["absb"], ["bnA"])
            V(lambda e: e.bn_stats(s_bnA[:, 6:12], acopy[:, 512:1024]), ["absb"], ["bnA"])
            combine1024("mua", E=GC, scd=gsc, P="g", msqt=s_gmsq, bnk="bnA", bnt=s_bnA)
            babylon("a", 3, "inva", seed=s_aseed[:, kk:kk + 1], E=GC, scd=gsc, P="g")
            GC(lambda e: e.tensor_scalar_mul(gsc["t1"][:], gsc["mua"][:], -1.0), ["gmua"], ["gt1"])
            GC(lambda e: e.tensor_mul(gsc["bA"][:], gsc["t1"][:], gsc["inva"][:]), ["gt1", "ginva"], ["gbA"])
            S(lambda e: e.activation(t3[:], acopy, AF.Identity,
                                     bias=gsc["bA"][:], scale=gsc["inva"][:]),
              ["absb", "gbA", "ginva"], ["t3buf"])
            V((lambda par=par: (lambda e: e.tensor_add(x1b[par][:], x0b[par][:], t3[:])))(),
              [xk, "t3buf"], [f"x1b{par}"])
            if DBG and kk == 0:
                SY((lambda par=par: (lambda e: e.dma_start(out=d_dbg_x1, in_=x1b[par][:])))(),
                   [f"x1b{par}"], ["dbg5"], sem="sO")

        def x1prep(k):
            kk = k - NP
            par = k % 2
            S((lambda par=par: (lambda e: e.activation(x1c[par][:], x1b[par][:], AF.Identity)))(),
              [f"x1b{par}"], [f"x1c{par}"])
            V((lambda par=par: (lambda e: e.bn_stats(s_bn[:, 0:6], x1b[par][:, 0:512])))(),
              [f"x1b{par}"], ["bn"])
            V((lambda par=par: (lambda e: e.bn_stats(s_bn[:, 6:12], x1b[par][:, 512:1024])))(),
              [f"x1b{par}"], ["bn"])
            combine1024("mu1")
            babylon("x1", 6 if kk == 0 else 3, "inv1")
            V(lambda e: e.scalar_tensor_tensor(sc["t1"][:], sc["inv1"][:], 1.0, sc["inv1"][:],
                                               op0=ALU.mult, op1=ALU.mult), ["inv1"], ["t1"])
            V(lambda e: e.tensor_mul(sc["vb"][:], sc["va"][:], sc["t1"][:]), ["va", "t1"], ["vb"])
            V(lambda e: e.tensor_scalar(sc["t4"][:], sc["vb"][:], -0.5, 1.5 - 0.5 * EPS,
                                        op0=ALU.mult, op1=ALU.add), ["vb"], ["t4"])
            V((lambda par=par: (lambda e: e.tensor_mul(alph[par][:], sc["inv1"][:], sc["t4"][:])))(),
              ["inv1", "t4"], [f"al{par}"])
            V((lambda par=par: (lambda e: e.reciprocal(auxc[par][:, 0:1], alph[par][:])))(),
              [f"al{par}"], [f"auxc{par}"])
            V((lambda par=par: (lambda e: e.tensor_scalar_mul(auxc[par][:, 1:2], sc["mu1"][:], -1.0)))(),
              ["mu1"], [f"auxc{par}"])
            T((lambda par=par: (lambda e: e.matmul(pX[0:2, 0:128], auxc[par][:], c_idf[:],
                                                   start=True, stop=True)))(),
              [f"auxc{par}", "consts"], ["pX"])
            V((lambda par=par: (lambda e: e.tensor_copy(auxL1[par][:], pX[0:2, 0:128])))(),
              ["pX"], [f"auxL1{par}"])

        def ffn_mm1(k, qs):
            kk = k - NP
            par = k % 2
            if 0 in qs:
                T((lambda par=par: (lambda e: [e.transpose(ptr[:, 128 * jj:128 * (jj + 1)],
                                               x1c[par][:, 128 * jj:128 * (jj + 1)], c_id[:])
                                               for jj in range(8)][-1]))(),
                  [f"x1c{par}", "consts"], ["ptr"])
                V(lambda e: e.tensor_copy(x1cT, ptr[:]), ["ptr"], ["scr0L"])
            for q in qs:
                pb = q % 2
                T((lambda q=q, pb=pb, par=par: (lambda e: (
                    [e.matmul(pv[pb][:], x1cT[:, 128 * jj:128 * (jj + 1)],
                              T1[:, jj * H + 512 * q: jj * H + 512 * (q + 1)],
                              start=(jj == 0), stop=False) for jj in range(8)],
                    e.matmul(pv[pb][:], auxL1[par][:],
                             m1mov[0:2, 512 * q:512 * (q + 1)],
                             start=False, stop=True))[-1]))(),
                  ["scr0L", "T1", "m1movb", f"auxL1{par}", "consts"], [f"pv{pb}"])
                S((lambda q=q, pb=pb, par=par: (lambda e: e.activation(
                    gb[:, 512 * q:512 * (q + 1)], pv[pb][:], AF.Gelu,
                    scale=alph[par][:], accum_out=gaccq[q][:])))(),
                  [f"pv{pb}", f"al{par}"], ["gbuf", f"gq{q}"])
                if q % 2 == 1:
                    grp = q // 2
                    T((lambda grp=grp: (lambda e: [e.transpose(
                        ptr[:, 128 * jj:128 * (jj + 1)],
                        gb[:, 1024 * grp + 128 * jj: 1024 * grp + 128 * (jj + 1)], c_id[:])
                        for jj in range(8)][-1]))(), ["gbuf", "consts"], ["ptr"])
                    V((lambda grp=grp: (lambda e: e.tensor_copy(
                        zT[:, 1024 * grp:1024 * (grp + 1)], ptr[:])))(), ["ptr"], ["zTb"])
            if 7 in qs:
                if DBG and kk == 0:
                    SY(lambda e: e.dma_start(out=d_dbg_g, in_=gb[:]), ["gbuf"], ["dbg6"], sem="sO")
                S(lambda e: e.activation(sqjunk, gb[:], AF.Square, accum_out=s_gsq[:]),
                  ["gbuf"], ["scr0L", "scr0R", "scr1L", "scr1R", "gsq"])

        def ffn_mm2(k):
            kk = k - NP
            par = k % 2
            for hh in range(2):
                T((lambda hh=hh: (lambda e: [e.matmul(
                    pF[:, 512 * hh:512 * (hh + 1)],
                    zT[:, 128 * cc:128 * (cc + 1)],
                    T2[:, cc * D + 512 * hh: cc * D + 512 * (hh + 1)],
                    start=(cc == 0), stop=False) for cc in range(32)][-1]))(),
                  ["zTb", "T2", "consts"], ["pF"])
            V(lambda e: e.tensor_add(s_msq[:, 0:1], gaccq[0][:], gaccq[1][:]), ["gq0", "gq1", "gsq"], ["msq"])
            V(lambda e: e.tensor_add(s_msq[:, 1:2], gaccq[2][:], gaccq[3][:]), ["gq2", "gq3"], ["msq"])
            V(lambda e: e.tensor_add(s_msq[:, 2:3], gaccq[4][:], gaccq[5][:]), ["gq4", "gq5"], ["msq"])
            V(lambda e: e.tensor_add(s_msq[:, 3:4], gaccq[6][:], gaccq[7][:]), ["gq6", "gq7"], ["msq"])
            V(lambda e: e.tensor_add(s_msq[:, 0:1], s_msq[:, 0:1], s_msq[:, 1:2]), ["msq"], ["msq"])
            V(lambda e: e.tensor_add(s_msq[:, 2:3], s_msq[:, 2:3], s_msq[:, 3:4]), ["msq"], ["msq"])
            V(lambda e: e.tensor_add(sc["ms"][:], s_msq[:, 0:1], s_msq[:, 2:3]), ["msq"], ["ms"])
            V(lambda e: e.tensor_scalar_mul(sc["mu2"][:], sc["ms"][:], 1.0 / 4096.0), ["ms"], ["mu2"])
            V(lambda e: e.tensor_mul(sc["t1"][:], sc["mu2"][:], sc["mu2"][:]), ["mu2"], ["t1"])
            V(lambda e: e.scalar_tensor_tensor(sc["va"][:], s_gsq[:], 1.0 / 4096.0, sc["t1"][:],
                                               op0=ALU.mult, op1=ALU.subtract), ["gsq", "t1"], ["va"])
            V(lambda e: e.tensor_scalar(sc["x"][:], sc["va"][:], 0.0, EPS, op0=ALU.max, op1=ALU.add), ["va"], ["x"])
            babylon_sqrt("g", 6 if kk == 0 else 4, "inv2")
            if DBG and kk == 0:
                V(lambda e: (
                    e.tensor_copy(sdbg_sc[:, 5:6], sc["inv2"][:]),
                    e.tensor_copy(sdbg_sc[:, 6:7], sc["mu2"][:]),
                    e.tensor_copy(sdbg_sc[:, 7:8], bab["g"][:]),
                    e.tensor_copy(sdbg_sc[:, 8:9], sc["x"][:]),
                    e.tensor_copy(sdbg_sc[:, 9:10], sc["va"][:]),
                    e.tensor_copy(sdbg_sc[:, 10:11], s_gsq[:]),
                    e.tensor_copy(sdbg_sc[:, 11:12], gaccq[0][:]))[-1],
                  ["inv2", "mu2", "bab_g", "x", "va", "gsq", "gq0"], ["dbgsc"])
                SY((lambda par=par: (lambda e: e.dma_start(out=d_dbg_aux, in_=auxL1[par][:])))(),
                   [f"auxL1{par}"], ["dbga1"], sem="sO")
                SY(lambda e: e.dma_start(out=d_dbg_m1, in_=m1mov[:]), ["m1movb"], ["dbga2"], sem="sO")
            V(lambda e: e.tensor_copy(auxc2[:, 0:1], bab["g"][:]), ["bab_g"], ["auxc2"])
            V(lambda e: e.tensor_scalar_mul(auxc2[:, 1:2], sc["mu2"][:], -1.0), ["mu2"], ["auxc2"])
            T(lambda e: e.matmul(pX[0:2, 128:256], auxc2[:], c_idf[:], start=True, stop=True),
              ["auxc2", "consts"], ["pX"])
            V(lambda e: e.tensor_copy(auxL2[:], pX[0:2, 128:256]), ["pX"], ["auxL2b"])
            for hh in range(2):
                T((lambda hh=hh: (lambda e: e.matmul(
                    pF[:, 512 * hh:512 * (hh + 1)], auxL2[:],
                    m2mov[0:2, 512 * hh:512 * (hh + 1)],
                    start=False, stop=True)))(),
                  ["auxL2b", "m2movb", "consts"], ["pF"])

        def ffn_tail(k):
            kk = k - NP
            par = k % 2
            S(lambda e: e.activation(fcopy, pF[:], AF.Identity), ["pF"], ["fcop"])
            V(lambda e: e.bn_stats(s_bn[:, 0:6], fcopy[:, 0:512]), ["fcop"], ["bn"])
            V(lambda e: e.bn_stats(s_bn[:, 6:12], fcopy[:, 512:1024]), ["fcop"], ["bn"])
            combine1024("muf")
            V(lambda e: e.tensor_mul(sc["muf"][:], sc["muf"][:], sc["inv2"][:]), ["muf", "inv2"], ["muf"])
            V(lambda e: e.tensor_mul(sc["t1"][:], sc["inv2"][:], sc["inv2"][:]), ["inv2"], ["t1"])
            V(lambda e: e.tensor_mul(sc["va"][:], sc["va"][:], sc["t1"][:]), ["va", "t1"], ["va"])
            V(lambda e: e.tensor_scalar(sc["x"][:], sc["va"][:], 0.0, EPS, op0=ALU.max, op1=ALU.add), ["va"], ["x"])
            babylon_sqrt("f", 6 if kk == 0 else 4, "invf")
            V(lambda e: e.scalar_tensor_tensor(sc["bF"][:], sc["muf"][:], -1.0, sc["invf"][:],
                                               op0=ALU.mult, op1=ALU.mult), ["muf", "invf"], ["bF"])
            V(lambda e: e.tensor_mul(sc["t2"][:], sc["inv2"][:], sc["invf"][:]), ["inv2", "invf"], ["t2"])
            S(lambda e: e.activation(t3f[:], fcopy, AF.Identity,
                                     bias=sc["bF"][:], scale=sc["t2"][:]),
              ["fcop", "bF", "t2"], ["t3fbuf"])
            V((lambda par=par: (lambda e: e.tensor_add(x1b[par][:], x1b[par][:], t3f[:])))(),
              ["t3fbuf"], [f"x1b{par}"])
            SY((lambda par=par, kk=kk: (lambda e: e.dma_start(
                out=d_out[128 * kk:128 * (kk + 1), :], in_=x1b[par][:])))(),
               [f"x1b{par}"], ["outd"], sem="sO")


        p1slots = [(0, j) for j in range(32)] + [(1, j) for j in range(32)]
        for i, (w, j) in enumerate(p1slots):
            if i % 4 == 0 and i // 4 < NP:
                k = i // 4
                pa_stats_pre(k)
                pa_mms(k)
            wdma(i, w1src(j) if w == 0 else w2src(j))
            if i >= 4:
                wabs(i - 4, *p1slots[i - 4])
            if i == 35:
                hi_mean(0, s_th1, s_q1, "th1", "q1", s_thpp, s_nthpp, s_q1pp, s_g1q, s_g1c,
                        "thpp", "nthpp", "q1pp", "g1q")
        for i in range(60, 64):
            wabs(i, *p1slots[i])
        hi_mean(1, s_th2, s_q2, "th2", "q2", s_thpp2, s_nthpp2, s_q2pp, s_g2q, s_g2c,
                "thpp2", "nthpp2", "q2pp", "g2q")
        for i in range(32):
            wdma(64 + i, w1src(i))
            if i >= 4:
                wtern1(64 + i - 4, i - 4)
        for i in range(28, 32):
            wtern1(64 + i, i)
        if DBG:
            V(lambda e: (
                e.tensor_copy(sdbg_th[0:1, 0:1], s_th1[:]),
                e.tensor_copy(sdbg_th[0:1, 1:2], s_q1[:]),
                e.tensor_copy(sdbg_th[0:1, 2:3], s_th2[:]),
                e.tensor_copy(sdbg_th[0:1, 3:4], s_q2[:]),
                e.tensor_copy(sdbg_th[0:1, 4:5], s_thpp[0:1, :]),
                e.tensor_copy(sdbg_th[0:1, 5:6], s_nthpp[0:1, :]))[-1],
              ["th1", "th2", "q1", "q2", "thpp", "nthpp"], ["dbgth"])
            SY(lambda e: e.dma_start(out=d_dbg_th, in_=sdbg_th[:]), ["dbgth"], ["dbgth"], sem="sO")
        if DBG:
            V(lambda e: (
                e.tensor_copy(sdbg_th[0:1, 0:1], s_th1[:]),
                e.tensor_copy(sdbg_th[0:1, 1:2], s_q1[:]),
                e.tensor_copy(sdbg_th[0:1, 2:3], s_th2[:]),
                e.tensor_copy(sdbg_th[0:1, 3:4], s_q2[:]),
                e.tensor_copy(sdbg_th[0:1, 4:5], s_thpp[0:1, :]),
                e.tensor_copy(sdbg_th[0:1, 5:6], s_nthpp[0:1, :]))[-1],
              ["th1", "th2", "q1", "q2", "thpp", "nthpp"], ["dbgth"])
            SY(lambda e: e.dma_start(out=d_dbg_th, in_=sdbg_th[:]), ["dbgth"], ["dbgth"], sem="sO")
        for q in range(8):
            if q % 2 == 0:
                SY((lambda q=q: (lambda e: e.dma_start(
                    out=s_brow, in_=d_b1r[0:1, 512 * q:512 * (q + 2)])))(),
                   [], ["t3buf"])
            T((lambda q=q: (lambda e: [
                e.matmul(pF[0:1, 0:512], s_b1vcb[:, jj:jj + 1],
                         T1[:, jj * H + 512 * q: jj * H + 512 * (q + 1)],
                         start=(jj == 0), stop=(jj == 7)) for jj in range(8)][-1]))(),
              ["T1", "b1vcb"], ["pF"])
            V((lambda q=q: (lambda e: e.scalar_tensor_tensor(
                m1mov[0:1, 512 * q:512 * (q + 1)], pF[0:1, 0:512], s_q1[:],
                s_brow[0:1, 512 * (q % 2):512 * (q % 2 + 1)], op0=ALU.mult, op1=ALU.add)))(),
              ["pF", "q1", "t3buf"], ["m1movb"])
        V(lambda e: [e.tensor_scalar_mul(
            T1[:, jj * H:(jj + 1) * H], T1[:, jj * H:(jj + 1) * H], s_g1q[:, jj:jj + 1])
            for jj in range(8)][-1], ["g1q"], ["T1"])
        for sl in range(8):
            T((lambda sl=sl: (lambda e: [e.matmul(
                pF[0:1, 0:512], c_ocol[:],
                T1[:, jj * H + 512 * sl: jj * H + 512 * (sl + 1)],
                start=(jj == 0), stop=(jj == 7)) for jj in range(8)][-1]))(),
              ["T1", "consts"], ["pF"])
            V(lambda e: e.tensor_copy(absb2[0:1, 0:512], pF[0:1, 0:512]), ["pF"], ["fcop"])
            SY((lambda sl=sl: (lambda e: e.dma_start(
                out=m1mov[1:2, 512 * sl:512 * (sl + 1)], in_=absb2[0:1, 0:512])))(),
               ["fcop"], ["m1movb"])
        pa_stats(NP)
        pa_stats(NP + 1)
        pa_mms(NP)
        pa_post(NP)
        x1prep(NP)
        ffn_mm1(NP, (0, 1, 2, 3))
        pa_mms(NP + 1)
        pa_post(NP + 1)
        ffn_mm1(NP, (4, 5, 6, 7))
        pa_stats(NP + 2)
        for j in range(32):
            buf, names = wstg[j % 2]
            SY((lambda b=buf, j=j: (lambda e: e.dma_start(
                out=b[:, 0:512], in_=d_w2t[128 * j:128 * (j + 1), 0:512])))(),
               [], [names[0]])
            SD((lambda b=buf, j=j: (lambda e: e.dma_start(
                out=b[:, 512:1024], in_=d_w2t[128 * j:128 * (j + 1), 512:1024])))(),
               [], [names[1]])
            if j >= 1:
                wtern2(j - 1)
        wtern2(31)
        SY(lambda e: e.dma_start(out=s_brow, in_=d_b2r[0:1, :]), [], ["t3buf"])
        for q in range(2):
            T((lambda q=q: (lambda e: [
                e.matmul(pF[0:1, 0:512], s_b2vcb[:, cc:cc + 1],
                         T2[:, cc * D + 512 * q: cc * D + 512 * (q + 1)],
                         start=(cc == 0), stop=(cc == 31)) for cc in range(32)][-1]))(),
              ["T2", "b2vcb"], ["pF"])
            V((lambda q=q: (lambda e: e.scalar_tensor_tensor(
                m2mov[0:1, 512 * q:512 * (q + 1)], pF[0:1, 0:512], s_q2[:],
                s_brow[0:1, 512 * q:512 * (q + 1)], op0=ALU.mult, op1=ALU.add)))(),
              ["pF", "q2", "t3buf"], ["m2movb"])
        V(lambda e: [e.tensor_scalar_mul(
            T2[:, cc * D:(cc + 1) * D], T2[:, cc * D:(cc + 1) * D], s_g2q[:, cc:cc + 1])
            for cc in range(32)][-1], ["g2q"], ["T2"])
        for sl in range(2):
            T((lambda sl=sl: (lambda e: [e.matmul(
                pF[0:1, 0:512], c_ocol[:],
                T2[:, cc * D + 512 * sl: cc * D + 512 * (sl + 1)],
                start=(cc == 0), stop=(cc == 31)) for cc in range(32)][-1]))(),
              ["T2", "consts"], ["pF"])
            V(lambda e: e.tensor_copy(absb2[0:1, 0:512], pF[0:1, 0:512]), ["pF"], ["fcop"])
            SY((lambda sl=sl: (lambda e: e.dma_start(
                out=m2mov[1:2, 512 * sl:512 * (sl + 1)], in_=absb2[0:1, 0:512])))(),
               ["fcop"], ["m2movb"])
        if DBG:
            SY(lambda e: e.dma_start(out=d_dbg_t1, in_=T1[:, 0:1024]), ["T1"], ["dbg1"], sem="sO")
            SY(lambda e: e.dma_start(out=d_dbg_bt, in_=m1mov[0:1, :]), ["m1movb"], ["dbg2"], sem="sO")
            SY(lambda e: e.dma_start(out=d_dbg_b2, in_=m2mov[0:1, :]), ["m2movb"], ["dbg2b"], sem="sO")
            SY(lambda e: e.dma_start(out=d_dbg_t2, in_=T2[:, 0:1024]), ["T2"], ["dbg2c"], sem="sO")

        ffn_mm2(NP)
        x1prep(NP + 1)
        ffn_tail(NP)
        for k in range(NP + 1, NP + NT):
            ffn_mm1(k, (0, 1, 2, 3))
            if k + 1 < NP + NT:
                pa_mms(k + 1)
                pa_post(k + 1)
            ffn_mm1(k, (4, 5, 6, 7))
            if k + 2 < NP + NT:
                pa_stats(k + 2)
            ffn_mm2(k)
            if k + 1 < NP + NT:
                x1prep(k + 1)
            ffn_tail(k)
        with nc.Block() as block:
            sems = {n: stack.enter_context(nc.semaphore(n)) for n in sems_names}

            @block.sync
            def _(sync):
                em.replay("SY", sync, sems)
                sync.wait_ge(sems["sO"], em.cnt["sO"])

            @block.tensor
            def _(tensor):
                em.replay("T", tensor, sems)

            @block.scalar
            def _(scalar):
                em.replay("S", scalar, sems)

            @block.vector
            def _(vector):
                em.replay("V", vector, sems)

            @block.gpsimd
            def _(gpsimd):
                em.replay("GP", gpsimd, sems)

    return nc


_NC_CACHE = None
LAST_RESULTS = None


def _get_nc():
    global _NC_CACHE
    if _NC_CACHE is None:
        _NC_CACHE = build_program()
    return _NC_CACHE


def kernel(L0_attn_x, L0_ffn_ln1_g, L0_ffn_ln1_b, L0_ffn_w1, L0_ffn_b1,
           L0_ffn_ln2_g, L0_ffn_ln2_b, L0_ffn_w2, L0_ffn_b2):
    x = np.asarray(L0_attn_x, dtype=np.float32)
    w1 = np.asarray(L0_ffn_w1, dtype=np.float32)
    w2 = np.asarray(L0_ffn_w2, dtype=np.float32)
    bf = ml_dtypes.bfloat16

    w1t = np.ascontiguousarray(w1.T)
    w2t = np.ascontiguousarray(w2.T)
    g1c = np.ascontiguousarray(np.asarray(L0_ffn_ln1_g, np.float32).reshape(8, 128).T)
    b1vc = np.ascontiguousarray(np.asarray(L0_ffn_ln1_b, np.float32).reshape(8, 128).T)
    g2c = np.ascontiguousarray(np.asarray(L0_ffn_ln2_g, np.float32).reshape(32, 128).T)
    b2vc = np.ascontiguousarray(np.asarray(L0_ffn_ln2_b, np.float32).reshape(32, 128).T)
    b1r = np.asarray(L0_ffn_b1, np.float32).reshape(1, H)
    b2r = np.asarray(L0_ffn_b2, np.float32).reshape(1, D)

    tri = np.tril(np.ones((128, 128), np.float32))
    consts = dict(
        triu=np.ascontiguousarray(tri.T).astype(bf),
        stril=np.ascontiguousarray(np.tril(np.ones((128, 128), np.float32), -1)).astype(bf),
        ones128=np.ones((128, 128), bf),
        ident=np.eye(128, dtype=bf),
        onesrow=np.ones((1, 128), bf),
        onesrowf=np.ones((1, 128), np.float32),
        identf=np.eye(128, dtype=np.float32),
        onescol=np.ones((128, 1), bf),
        w1t=w1t, w2t=w2t, g1c=g1c, b1vc=b1vc, g2c=g2c, b2vc=b2vc,
        b1r=b1r, b2r=b2r,
    )

    in_maps = []
    for c in range(8):
        b, hhalf = divmod(c, 2)
        xm = np.ascontiguousarray(x[b, 2048 * hhalf: 2048 * (hhalf + 1)])
        xp = np.ascontiguousarray(x[b, 0:2048]) if hhalf else np.zeros((2048, D), np.float32)
        tglob = 2048 * hhalf + (np.arange(NT)[None, :] * 128 + np.arange(128)[:, None])
        counts = (tglob + 1).astype(np.float32)
        aseed = (1.0 / np.sqrt(1.0 / counts + EPS)).astype(np.float32)
        in_maps.append(dict(xm=xm, xp=xp, counts=counts, aseed=aseed, **consts))

    nc = _get_nc()
    res = run_bass_kernel_spmd(nc, in_maps, core_ids=list(range(8)))
    global LAST_RESULTS
    LAST_RESULTS = res.results
    out = np.empty((4, 4096, D), dtype=np.float32)
    for c in range(8):
        b, hhalf = divmod(c, 2)
        out[b, 2048 * hhalf: 2048 * (hhalf + 1)] = res.results[c]["out"]
    return out
